# version 36
# baseline (speedup 1.0000x reference)
import numpy as np
import concourse.bass as bass
import concourse.mybir as mybir
from concourse.bass_utils import run_bass_kernel_spmd

F32 = mybir.dt.float32
BF16 = mybir.dt.bfloat16
AF = mybir.ActivationFunctionType
ALU = mybir.AluOpType
AX = mybir.AxisListType

B, TSEQ, D = 2, 8192, 1024
H, DK, DV = 8, 128, 128
INTER, RANK = 4096, 256
KC = D // 128
TT = 512
NT = 4
TCORE = TT * NT
CPT = TT // 128
PRE = TSEQ - TCORE
PREG = PRE // TT
EPS = 1e-5
IC = INTER // 128

Q_OFF, K_OFF, V_OFF, O_OFF = 0, 4096, 8192, 12288
X2_OFF = 0
QS_OFF = 4096
G1REP_OFF = 9216
G1ROW_OFF = 10240
ONT_H_OFF = 12288
ONT_L_OFF = 14336


def _h(g):
    return g.tensor if hasattr(g, "tensor") else g


def slot_of(h):
    return (h % 2) * 4 + (h // 2)


class Em:

    def __init__(self):
        self.ops = {k: [] for k in "TVSGY"}
        self.cnt = {}
        self.lastwait = {}
        self.writer = {}
        self.readers = {}

    def op(self, eng, fn, reads=(), writes=(), dma=False):
        need = {}
        toks = []
        for r in reads:
            t = self.writer.get(r)
            if t is not None:
                toks.append(t)
        for w in writes:
            t = self.writer.get(w)
            if t is not None:
                toks.append(t)
            toks.extend(self.readers.get(w, ()))
        for (sem, val) in toks:
            if need.get(sem, 0) < val:
                need[sem] = val
        for sem, val in need.items():
            key = (eng, sem)
            if self.lastwait.get(key, 0) >= val:
                continue
            self.lastwait[key] = val
            self.ops[eng].append(("w", sem, val))
        if dma:
            sem = "q" + (writes[0] if writes else "r" + reads[0])
        else:
            sem = "s" + eng
        inc = 16 if dma else 1
        self.cnt[sem] = self.cnt.get(sem, 0) + inc
        tok = (sem, self.cnt[sem])
        self.ops[eng].append(("i", fn, sem, inc))
        for w in writes:
            self.writer[w] = tok
            self.readers[w] = []
        for r in reads:
            self.readers.setdefault(r, []).append(tok)
        return tok


def build_nc():
    nc = bass.Bass("TRN2", target_bir_lowering=False)

    xin = nc.dram_tensor("xin", [TSEQ, D], F32, kind="ExternalInput")
    y_t = nc.dram_tensor("y", [TCORE, D], F32, kind="ExternalOutput")
    wq = nc.dram_tensor("wq", [D, D], F32, kind="ExternalInput")
    wk = nc.dram_tensor("wk", [D, D], F32, kind="ExternalInput")
    wv = nc.dram_tensor("wv", [D, D], F32, kind="ExternalInput")
    wo = nc.dram_tensor("wo", [D, D], F32, kind="ExternalInput")
    wg = nc.dram_tensor("wg", [D, INTER], F32, kind="ExternalInput")
    wu = nc.dram_tensor("wu", [D, INTER], F32, kind="ExternalInput")
    wd = nc.dram_tensor("wd", [INTER, D], F32, kind="ExternalInput")
    u1 = nc.dram_tensor("u1", [D, RANK], F32, kind="ExternalInput")
    v1 = nc.dram_tensor("v1", [RANK, INTER], F32, kind="ExternalInput")
    b1 = nc.dram_tensor("b1", [INTER], F32, kind="ExternalInput")
    u2 = nc.dram_tensor("u2", [INTER, RANK], F32, kind="ExternalInput")
    v2 = nc.dram_tensor("v2", [RANK, D], F32, kind="ExternalInput")
    b2 = nc.dram_tensor("b2", [D], F32, kind="ExternalInput")
    g1 = nc.dram_tensor("g1", [D], F32, kind="ExternalInput")
    g2 = nc.dram_tensor("g2", [D], F32, kind="ExternalInput")
    go = nc.dram_tensor("go", [DV], F32, kind="ExternalInput")

    SPLIT_W = {"wq": wq, "wk": wk, "wv": wv, "wo": wo, "wg": wg, "wu": wu,
               "wd": wd, "v1": v1, "u2": u2}
    dwh, dwl = {}, {}
    for nm, t in SPLIT_W.items():
        dwh[nm] = nc.dram_tensor("h_" + nm, list(t.shape), BF16)
        dwl[nm] = nc.dram_tensor("l_" + nm, list(t.shape), BF16)

    from contextlib import ExitStack
    with ExitStack() as ctx:
        e = ctx.enter_context
        sb = {}
        for name, shape, dt in [
            ("A", [128, 16384], F32),
            ("x0", [128, 4096], F32), ("x1", [128, 4096], F32),
            ("hhb", [128, 8192], BF16),
            ("stage", [128, 4096], F32),
            ("wa", [128, 8192], BF16), ("wb", [128, 8192], BF16),
            ("sg", [128, 1024], F32),
            ("Ssb", [128, 1024], F32),
            ("att", [128, 1024], F32), ("on_", [128, 1024], F32),
            ("Pb", [128, 2048], BF16),
            ("sp0", [128, 512], F32), ("sp1", [128, 512], F32),
            ("ro", [128, 8], F32), ("ro2", [128, 8], F32),
            ("mask", [128, 128], F32), ("ident", [128, 128], F32),
            ("ta0", [128, 512], F32), ("ta1", [128, 512], F32),
            ("tb", [128, 512], F32), ("tc", [128, 512], F32),
            ("rrep", [128, 512], F32), ("mrep", [128, 512], F32),
            ("rrep2", [128, 512], F32),
            ("onesc", [128, 1], F32), ("onesr", [1, 128], F32),
            ("negc", [128, 1], F32), ("cu1c", [128, 2], F32),
            ("ceps", [128, 1], F32), ("cD", [128, 1], F32),
            ("cDV", [128, 1], F32), ("cNH", [128, 1], F32),
            ("g1c", [128, 8], F32), ("g2c", [128, 8], F32),
            ("b1c", [128, 32], F32), ("b2c", [128, 8], F32),
            ("gocol", [128, 1], F32),
            ("rq", [128, 4], F32), ("rcol", [128, 4], F32),
            ("Hb", [128, 3072], BF16),
        ]:
            sb[name] = e(nc.sbuf_tensor(name, shape, dt))
        ps = [e(nc.psum_tensor(f"ps{i}", [128, 512], F32)) for i in range(8)]

        A = sb["A"]

        def ap(g, off, dims):
            return bass.AP(_h(g), off, dims)

        def aap(off, dims):
            return bass.AP(_h(A), off, dims)

        def ab16(off2, dims2):
            fdims = [[s // 2, n] for (s, n) in dims2[:-1]] + [[1, dims2[-1][1] // 2]]
            return bass.AP(_h(A), off2 // 2, fdims).bitcast(BF16)

        def wf32(buf, off32, fdims32):
            bdims = ([[8192, 128]] + [[s * 2, n] for (s, n) in fdims32[:-1]]
                     + [[1, fdims32[-1][1] * 2]])
            return bass.AP(_h(sb[buf]), off32 * 2, bdims).bitcast(F32)

        rowa = ap(sb["mrep"], 0, [[512, 1], [1, 512]])
        rowb = ap(sb["rrep"], 0, [[512, 1], [1, 512]])
        rowc = ap(sb["tb"], 0, [[512, 1], [1, 512]])

        E = Em()

        def dma(eng, dst, src, reads=(), writes=()):
            return E.op(eng, lambda ee, d=dst, s=src: ee.dma_start(d, s),
                        reads=reads, writes=writes, dma=True)

        def mm(items, reads=(), writes=()):
            def fn(ee, items=tuple(items)):
                last = None
                for (o, l, r, s, p) in items:
                    last = ee.matmul(o, l, r, start=s, stop=p)
                return last
            return E.op("T", fn, reads=reads, writes=writes)

        def trs(items, reads=(), writes=()):
            def fn(ee, items=tuple(items)):
                last = None
                for (o, i, idn) in items:
                    last = ee.transpose(o, i, idn)
                return last
            return E.op("T", fn, reads=reads, writes=writes)

        def v_op(name, *args, reads=(), writes=()):
            def fn(ee, name=name, args=args):
                return getattr(ee, name)(*args)
            return E.op("V", fn, reads=reads, writes=writes)

        def s_act(out, in_, func, bias=0.0, scale=1.0, reads=(), writes=()):
            def fn(ee):
                return ee.activation(out, in_, func, bias=bias, scale=scale)
            return E.op("S", fn, reads=reads, writes=writes)

        def g_op(name, *args, reads=(), writes=()):
            def fn(ee, name=name, args=args):
                return getattr(ee, name)(*args)
            return E.op("G", fn, reads=reads, writes=writes)

        WAQ = ["waq0", "waq1", "waq2", "waq3"]
        WBQ = ["wbq0", "wbq1", "wbq2", "wbq3"]

        rawv = {"g1raw": (0, 8, g1), "g2raw": (1024, 8, g2),
                "b1raw": (2048, 32, b1), "b2raw": (3072, 8, b2)}
        for nm_, (off_, n_, t_) in rawv.items():
            dma("Y", ap(sb["stage"], off_, [[4096, n_], [1, 128]]),
                ap(t_, 0, [[128, n_], [1, 128]]), writes=[f"st{off_ // 1024}"])
        dma("Y", ap(sb["gocol"], 0, [[1, 128], [1, 1]]), ap(go, 0, [[1, 128], [1, 1]]), writes=["gocol"])
        dma("Y", aap(G1ROW_OFF, [[16384, 1], [1, 1024]]),
            ap(g1, 0, [[1024, 1], [1, 1024]]), writes=["A2"])

        ioA = ap(sb["ta0"], 0, [[512, 128], [1, 128]])
        ioB = ap(sb["tb"], 0, [[512, 128], [1, 1]])
        g_op("iota", ioA, [[1, 128]], None, 0, True, writes=["ta0"])
        g_op("iota", ioB, [[1, 1]], None, 1, True, writes=["tb"])
        g_op("memset", sb["onesc"][:, :], 1.0, writes=["onesc"])
        g_op("memset", sb["negc"][:, :], -1.0, writes=["negc"])
        g_op("memset", sb["onesr"][:, :], 1.0, writes=["onesr"])
        g_op("memset", sb["ceps"][:, :], EPS, writes=["ceps"])
        g_op("memset", sb["cD"][:, :], 1.0 / D, writes=["cD"])
        g_op("memset", sb["cDV"][:, :], 1.0 / DV, writes=["cDV"])
        g_op("memset", sb["cNH"][:, :], -0.5, writes=["cNH"])
        v_op("memset", aap(0, [[16384, 128], [1, 8192]]), 0.0,
             writes=["A0", "A1"])

        v_op("tensor_scalar", sb["ident"][:, :], ioA, ioB,
             None, ALU.is_equal, reads=["ta0", "tb"], writes=["ident"])
        v_op("tensor_scalar", sb["mask"][:, :], ioA, ioB,
             None, ALU.is_ge, reads=["ta0", "tb"], writes=["mask"])

        for (raw, dstn, n, bank) in [("g1raw", "g1c", 8, 0), ("g2raw", "g2c", 8, 1),
                                     ("b1raw", "b1c", 32, 2), ("b2raw", "b2c", 8, 3)]:
            off_, n_, _ = rawv[raw]
            trs([(ps[bank][0:128, 0:n], ap(sb["stage"], off_, [[4096, n_], [1, 128]]),
                  sb["ident"][0:n, 0:n])],
                reads=[f"st{off_ // 1024}", "ident"], writes=[f"ps{bank}"])
            v_op("tensor_copy", sb[dstn][:, :], ps[bank][0:128, 0:n],
                 writes=[dstn, f"ps{bank}"])

        for half in range(2):
            mm([(ps[4][:, :], sb["onesr"][:, :],
                 aap(G1ROW_OFF + half * 512, [[16384, 1], [1, 512]]), True, True)],
               reads=["onesr", "A2"], writes=["ps4"])
            v_op("tensor_copy", aap(G1REP_OFF + half * 512, [[16384, 128], [1, 512]]),
                 ps[4][:, :], writes=["A2", "ps4"])

        split_blocks = []
        for nm, t in SPLIT_W.items():
            tot = t.shape[0] * t.shape[1]
            nblk = tot // (128 * 4096)
            for bkt in range(nblk):
                split_blocks.append((nm, bkt))

        spc = [0]

        def emit_split_block(nm, bkt):
            off = bkt * 128 * 4096
            w_t = SPLIT_W[nm]
            if spc[0] % 2 == 0:
                wsrc = wf32("wa", 0, [[1, 4096]])
                inq = WAQ
            else:
                wsrc = ap(sb["x1"], 0, [[4096, 128], [1, 4096]])
                inq = ["x1spl"]
            spc[0] += 1
            dma("Y", wsrc, ap(w_t, off, [[4096, 128], [1, 4096]]), writes=inq)
            s_act(sb["wb"][:, 0:4096], wsrc,
                  AF.Copy, reads=inq, writes=["wbq0", "wbq1"])
            g_op("tensor_sub", sb["wb"][:, 4096:8192],
                 wsrc, sb["wb"][:, 0:4096],
                 reads=inq + ["wbq0", "wbq1"], writes=["wbq2", "wbq3"])
            dma("S", ap(dwh[nm], off, [[4096, 128], [1, 4096]]),
                sb["wb"][:, 0:4096], reads=["wbq0", "wbq1"], writes=[f"D{nm}h"])
            dma("S", ap(dwl[nm], off, [[4096, 128], [1, 4096]]),
                sb["wb"][:, 4096:8192], reads=["wbq2", "wbq3"], writes=[f"D{nm}l"])

        def load_hl(nm, c0, C, nkc, ncols_total, buf):
            q = WAQ if buf == "wa" else WBQ
            dma("Y", ap(sb[buf], 0, [[8192, 128], [C, nkc], [1, C]]),
                ap(dwh[nm], c0, [[ncols_total, 128], [128 * ncols_total, nkc], [1, C]]),
                reads=[f"D{nm}h"], writes=q[0:2])
            dma("Y", ap(sb[buf], 4096, [[8192, 128], [C, nkc], [1, C]]),
                ap(dwl[nm], c0, [[ncols_total, 128], [128 * ncols_total, nkc], [1, C]]),
                reads=[f"D{nm}l"], writes=q[2:4])

        def tri(out_ap, wh, wl, xh, xl):
            return [(out_ap, wh, xh), (out_ap, wh, xl), (out_ap, wl, xh)]

        def stats_rms(src, srcres, rowbank, bcbank, dst="rrep"):
            if isinstance(srcres, str):
                srcres = [srcres]
            for kc in range(KC):
                tsl = "ta0" if kc % 2 == 0 else "ta1"
                s_act(sb[tsl][:, :], src[:, kc * TT:(kc + 1) * TT], AF.Square,
                      reads=[srcres[kc % len(srcres)]], writes=[tsl])
                if kc == 0:
                    v_op("tensor_copy", sb["tb"][:, :], sb[tsl][:, :],
                         reads=[tsl], writes=["tb"])
                else:
                    v_op("tensor_add", sb["tb"][:, :], sb["tb"][:, :], sb[tsl][:, :],
                         reads=[tsl], writes=["tb"])
            mm([(ps[rowbank][0:1, :], sb["onesc"][:, :], sb["tb"][:, :], True, True)],
               reads=["onesc", "tb"], writes=[f"ps{rowbank}"])
            rowt = ap(sb["tc"], 0, [[512, 1], [1, 512]])
            s_act(rowt, ps[rowbank][0:1, :], AF.Ln,
                  bias=sb["ceps"][0:1, 0:1], scale=sb["cD"][0:1, 0:1],
                  reads=["ceps", "cD"], writes=["tc", f"ps{rowbank}"])
            rowd = ap(sb[dst], 0, [[512, 1], [1, 512]])
            s_act(rowd, rowt, AF.Exp, scale=sb["cNH"][0:1, 0:1],
                  reads=["tc", "cNH"], writes=[dst])
            mm([(ps[bcbank][:, :], sb["onesr"][:, :], rowd, True, True)],
               reads=["onesr", dst], writes=[f"ps{bcbank}"])
            items = []
            for c in range(CPT):
                items.append((ps[rowbank][:, c:c + 1],
                              ap(sb[dst], c * 128, [[512, 1], [1, 128]]),
                              sb["onesr"][0:1, 0:1], True, True))
            mm(items, reads=["onesr", dst], writes=[f"ps{rowbank}"])
            v_op("tensor_copy", sb["rcol"][:, :], ps[rowbank][:, 0:CPT],
                 writes=["rcol", f"ps{rowbank}"])
            v_op("tensor_copy", sb[dst][:, :], ps[bcbank][:, :],
                 writes=[dst, f"ps{bcbank}"])

        def stats_ln(src, srcres):
            if isinstance(srcres, str):
                srcres = [srcres]
            for kc in range(KC):
                if kc == 0:
                    v_op("tensor_copy", sb["tc"][:, :], src[:, 0:TT],
                         reads=[srcres[kc % len(srcres)]], writes=["tc"])
                else:
                    v_op("tensor_add", sb["tc"][:, :], sb["tc"][:, :],
                         src[:, kc * TT:(kc + 1) * TT],
                         reads=[srcres[kc % len(srcres)]], writes=["tc"])
            for kc in range(KC):
                tsl = "ta0" if kc % 2 == 0 else "ta1"
                s_act(sb[tsl][:, :], src[:, kc * TT:(kc + 1) * TT], AF.Square,
                      reads=[srcres[kc % len(srcres)]], writes=[tsl])
                if kc == 0:
                    v_op("tensor_copy", sb["tb"][:, :], sb[tsl][:, :],
                         reads=[tsl], writes=["tb"])
                else:
                    v_op("tensor_add", sb["tb"][:, :], sb["tb"][:, :], sb[tsl][:, :],
                         reads=[tsl], writes=["tb"])
            mm([(ps[0][0:1, :], sb["onesc"][:, :], sb["tc"][:, :], True, True)],
               reads=["onesc", "tc"], writes=["ps0"])
            mm([(ps[1][0:1, :], sb["onesc"][:, :], sb["tb"][:, :], True, True)],
               reads=["onesc", "tb"], writes=["ps1"])
            s_act(rowa, ps[0][0:1, :], AF.Copy, scale=sb["cD"][0:1, 0:1],
                  reads=["cD"], writes=["mrep", "ps0"])
            s_act(rowb, ps[1][0:1, :], AF.Copy, scale=sb["cD"][0:1, 0:1],
                  reads=["cD"], writes=["rrep", "ps1"])
            v_op("tensor_mul", rowc, rowa, rowa, reads=["mrep"], writes=["tb"])
            v_op("tensor_sub", rowb, rowb, rowc, reads=["tb"], writes=["rrep"])
            s_act(rowb, rowb, AF.Ln, bias=sb["ceps"][0:1, 0:1],
                  reads=["ceps"], writes=["rrep"])
            s_act(rowb, rowb, AF.Exp, scale=sb["cNH"][0:1, 0:1],
                  reads=["cNH"], writes=["rrep"])
            v_op("tensor_mul", rowc, rowa, rowb,
                 reads=["mrep", "rrep"], writes=["tb"])
            mm([(ps[2][:, :], sb["onesr"][:, :], rowb, True, True)],
               reads=["onesr", "rrep"], writes=["ps2"])
            mm([(ps[3][:, :], sb["onesr"][:, :], rowc, True, True)],
               reads=["onesr", "tb"], writes=["ps3"])
            v_op("tensor_copy", sb["rrep"][:, :], ps[2][:, :], writes=["rrep", "ps2"])
            v_op("tensor_copy", sb["mrep"][:, :], ps[3][:, :], writes=["mrep", "ps3"])

        def apply_ln(src, srcres, dst_ap_fn, dstres):
            if isinstance(srcres, str):
                srcres = [srcres]
            for kc in range(KC):
                tsl = "ta0" if kc % 2 == 0 else "ta1"
                v_op("tensor_mul", sb[tsl][:, :], src[:, kc * TT:(kc + 1) * TT],
                     sb["rrep"][:, :],
                     reads=[srcres[kc % len(srcres)], "rrep"], writes=[tsl])
                v_op("tensor_sub", dst_ap_fn(kc), sb[tsl][:, :],
                     sb["mrep"][:, :], reads=[tsl, "mrep"], writes=dstres)

        dma("Y", wf32("wa", 0, [[256, KC], [1, 256]]),
            ap(u1, 0, [[RANK, 128], [128 * RANK, KC], [1, 256]]),
            writes=["waq0", "waq1"])
        items = []
        for kc in range(KC):
            items.append((ps[6][0:1, 0:256], sb["negc"][:, :],
                          wf32("wa", kc * 256, [[1, 256]]),
                          kc == 0, kc == KC - 1))
        mm(items, reads=["negc", "waq0", "waq1"], writes=["ps6"])
        v_op("tensor_copy", ap(sb["sp0"], 0, [[512, 1], [1, 256]]),
             ps[6][0:1, 0:256], writes=["sp0", "ps6"])
        for j in range(2):
            trs([(ps[5 + 2 * j][:, 0:1],
                  ap(sb["sp0"], j * 128, [[512, 1], [1, 128]]),
                  sb["ident"][0:1, 0:1])],
                reads=["sp0", "ident"], writes=[f"ps{5 + 2 * j}"])
            v_op("tensor_copy", sb["cu1c"][:, j:j + 1], ps[5 + 2 * j][:, 0:1],
                 writes=["cu1c", f"ps{5 + 2 * j}"])

        sbi = [0]

        def pre_stage(g):
            for c in range(CPT):
                dma("Y", ap(sb["stage"], c * 1024, [[4096, 128], [1, 1024]]),
                    ap(xin, (g * TT + c * 128) * D, [[D, 128], [1, D]]),
                    writes=[f"st{c}"])

        pre_stage(0)
        for g in range(PREG):
            for pair in range(2):
                for cc in range(2):
                    c = pair * 2 + cc
                    s_act(ap(sb["x0"], c * 1024, [[4096, 128], [1, 1024]]),
                          ap(sb["stage"], c * 1024, [[4096, 128], [1, 1024]]),
                          AF.Square, reads=[f"st{c}"], writes=["x0"])
                v_op("reduce_sum", sb["rq"][:, 0:2],
                     ap(sb["x0"], pair * 2048, [[4096, 128], [1024, 2], [1, 1024]]),
                     AX.X, reads=["x0"], writes=["rq"])
                s_act(sb["rq"][:, 2:4], sb["rq"][:, 0:2], AF.Ln,
                      bias=sb["ceps"][:, 0:1], scale=sb["cD"][:, 0:1],
                      reads=["ceps", "cD"], writes=["rq"])
                s_act(sb["rq"][:, 0:2], sb["rq"][:, 2:4], AF.Exp,
                      scale=sb["cNH"][:, 0:1], reads=["cNH"], writes=["rq"])
                hpres = f"hhp{pair}"
                for cc in range(2):
                    c = pair * 2 + cc
                    v_op("scalar_tensor_tensor",
                         ap(sb["on_"], 0, [[1024, 128], [1, 1024]]),
                         ap(sb["stage"], c * 1024, [[4096, 128], [1, 1024]]),
                         sb["rq"][:, cc:cc + 1], aap(G1REP_OFF, [[16384, 128], [1, 1024]]),
                         ALU.mult, ALU.mult,
                         reads=[f"st{c}", "rq", "A2"], writes=["on_"])
                    s_act(sb["hhb"][:, c * 1024:(c + 1) * 1024],
                          ap(sb["on_"], 0, [[1024, 128], [1, 1024]]),
                          AF.Copy, reads=["on_"], writes=[hpres])
                    v_op("tensor_sub", sb["hhb"][:, 4096 + c * 1024:4096 + (c + 1) * 1024],
                         ap(sb["on_"], 0, [[1024, 128], [1, 1024]]),
                         sb["hhb"][:, c * 1024:(c + 1) * 1024],
                         reads=["on_"], writes=[hpres])
            if g + 1 < PREG:
                pre_stage(g + 1)
            for _ in range(3):
                if sbi[0] < len(split_blocks):
                    emit_split_block(*split_blocks[sbi[0]])
                    sbi[0] += 1
            for pair in range(2):
                hpres = f"hhp{pair}"
                for kc in range(KC):
                    b0 = 0 if kc % 2 == 0 else 2
                    for half in range(2):
                        if half == 0 and kc >= 4:
                            continue
                        bank = b0 + half
                        items = []
                        k = 0
                        for cc in range(2):
                            c = pair * 2 + cc
                            lh = sb["hhb"][:, c * 1024 + kc * 128:c * 1024 + (kc + 1) * 128]
                            ll = sb["hhb"][:, 4096 + c * 1024 + kc * 128:4096 + c * 1024 + (kc + 1) * 128]
                            rh = sb["hhb"][:, c * 1024 + half * 512:c * 1024 + half * 512 + 512]
                            rl = sb["hhb"][:, 4096 + c * 1024 + half * 512:4096 + c * 1024 + half * 512 + 512]
                            for (l, r) in ((lh, rh), (lh, rl), (ll, rh)):
                                items.append((ps[bank][:, :], l, r, k == 0, k == 5))
                                k += 1
                        mm(items, reads=[hpres], writes=[f"ps{bank}"])
                        res = "A0" if kc < 4 else "A1"
                        v_op("tensor_add",
                             aap(kc * 1024 + half * 512, [[16384, 128], [1, 512]]),
                             aap(kc * 1024 + half * 512, [[16384, 128], [1, 512]]),
                             ps[bank][:, :], writes=[res, f"ps{bank}"])

        while sbi[0] < len(split_blocks):
            emit_split_block(*split_blocks[sbi[0]])
            sbi[0] += 1

        for kc in range(4, KC):
            for j in range(4):
                bank = (kc * 4 + j) % 4
                trs([(ps[bank][:, 0:128],
                      aap(j * 1024 + kc * 128, [[16384, 128], [1, 128]]),
                      sb["ident"][:, :])],
                    reads=["A0", "ident"], writes=[f"ps{bank}"])
                v_op("tensor_copy",
                     aap(kc * 1024 + j * 128, [[16384, 128], [1, 128]]),
                     ps[bank][:, 0:128], writes=["A1", f"ps{bank}"])

        for h in range(H):
            buf = "wa" if h % 2 == 0 else "wb"
            q = WAQ if buf == "wa" else WBQ
            dma("Y", wf32(buf, 0, [[128, KC], [1, 128]]),
                ap(wv, h * 128, [[D, 128], [128 * D, KC], [1, 128]]), writes=[q[0]])
            dma("Y", wf32(buf, 2048, [[128, KC], [1, 128]]),
                ap(wk, h * 128, [[D, 128], [128 * D, KC], [1, 128]]), writes=[q[2]])
            for ds in range(KC):
                bank = ds % 2
                items = []
                for kc2 in range(KC):
                    items.append((ps[bank][:, 0:128],
                                  aap(kc2 * 1024 + ds * 128, [[16384, 128], [1, 128]]),
                                  wf32(buf, kc2 * 128, [[1, 128]]),
                                  kc2 == 0, kc2 == KC - 1))
                mm(items, reads=["A0", "A1", q[0]], writes=[f"ps{bank}"])
                v_op("tensor_copy", aap(V_OFF + ds * 128, [[16384, 128], [1, 128]]),
                     ps[bank][:, 0:128], writes=["A2", f"ps{bank}"])
            items = []
            for kc in range(KC):
                items.append((ps[2][:, 0:128],
                              wf32(buf, 2048 + kc * 128, [[1, 128]]),
                              aap(V_OFF + kc * 128, [[16384, 128], [1, 128]]),
                              kc == 0, kc == KC - 1))
            mm(items, reads=["A2", q[2]], writes=["ps2"])
            s = slot_of(h)
            sres = "Sb6" if s < 4 else "Sb7"
            v_op("tensor_copy", sb["Ssb"][:, s * 128:(s + 1) * 128], ps[2][:, 0:128],
                 writes=[sres, "ps2"])

        X0K = [f"x0k{i}" for i in range(KC)]
        X1K = [f"x1k{i}" for i in range(KC)]
        XG_SLOTS = ["sp0", "sp1", "ta0", "ta1"]

        def emit_loads(tl):
            for c in range(CPT):
                dma("Y", ap(sb["stage"], c * 1024, [[4096, 128], [1, 1024]]),
                    ap(xin, (PRE + tl * TT + c * 128) * D, [[D, 128], [1, D]]),
                    writes=[f"st{c}"])

        def make_xg(xsrc, xres_fn, gvec, gres):
            for kc in range(KC):
                spl = XG_SLOTS[kc % 4]
                v_op("tensor_scalar", sb[spl][:, :],
                     xsrc[:, kc * TT:(kc + 1) * TT], gvec[:, kc:kc + 1],
                     None, ALU.mult, reads=[xres_fn(kc), gres], writes=[spl])
                s_act(sb["hhb"][:, kc * TT:(kc + 1) * TT], sb[spl][:, :],
                      AF.Copy, reads=[spl], writes=["hhp0", "hhp1"])
                v_op("tensor_sub", sb["hhb"][:, 4096 + kc * TT:4096 + (kc + 1) * TT],
                     sb[spl][:, :], sb["hhb"][:, kc * TT:(kc + 1) * TT],
                     reads=[spl], writes=["hhp0", "hhp1"])

        def emit_intr_xg(tl):
            for c in range(CPT):
                for kcg in range(2):
                    bank = (c * 2 + kcg) % 4
                    items = []
                    for j in range(4):
                        kc = kcg * 4 + j
                        items.append((ps[bank][:, j * 128:(j + 1) * 128],
                                      ap(sb["stage"], c * 1024 + kc * 128, [[4096, 128], [1, 128]]),
                                      sb["ident"][:, :]))
                    trs(items, reads=[f"st{c}", "ident"], writes=[f"ps{bank}"])
                    v_op("tensor_copy",
                         ap(sb["x0"], kcg * 4 * TT + c * 128, [[4096, 128], [TT, 4], [1, 128]]),
                         ps[bank][:, :],
                         writes=[f"x0k{kcg * 4 + j}" for j in range(4)] + [f"ps{bank}"])
            make_xg(sb["x0"], lambda kc: f"x0k{kc}", sb["g1c"], "g1c")

        def hh_h(kc):
            return sb["hhb"][:, kc * TT:(kc + 1) * TT]

        def hh_l(kc):
            return sb["hhb"][:, 4096 + kc * TT:4096 + (kc + 1) * TT]

        def emit_qkv():
            for (nm, dst_off, dst_res, kind) in [
                ("wq", Q_OFF, "A0", "feat"), ("wk", K_OFF, "A1", "feat"),
                ("wv", V_OFF, "A2", "tok"),
            ]:
                for cb in range(2):
                    buf = "wa" if (cb % 2 == 0) else "wb"
                    q = WAQ if buf == "wa" else WBQ
                    load_hl(nm, cb * 512, 512, KC, D, buf)
                    if kind == "feat":
                        for oc_in in range(4):
                            bank = oc_in % 2
                            items = []
                            k = 0
                            for kc in range(KC):
                                wh = ap(sb[buf], kc * 512 + oc_in * 128, [[8192, 128], [1, 128]])
                                wl = ap(sb[buf], 4096 + kc * 512 + oc_in * 128, [[8192, 128], [1, 128]])
                                for (l, r) in ((wh, hh_h(kc)), (wh, hh_l(kc)), (wl, hh_h(kc))):
                                    items.append((ps[bank][:, :], l, r, k == 0, k == 3 * KC - 1))
                                    k += 1
                            mm(items, reads=["hhp0", "hhp1"] + q, writes=[f"ps{bank}"])
                            oc = cb * 4 + oc_in
                            v_op("tensor_tensor",
                                 aap(dst_off + oc * TT, [[16384, 128], [1, TT]]),
                                 ps[bank][:, :], sb["rrep2"][:, :], ALU.mult,
                                 reads=["rrep2"], writes=[dst_res, f"ps{bank}"])
                    else:
                        for c in range(CPT):
                            bank = c % 2
                            items = []
                            k = 0
                            for kc in range(KC):
                                lh = sb["hhb"][:, kc * TT + c * 128:kc * TT + (c + 1) * 128]
                                ll = sb["hhb"][:, 4096 + kc * TT + c * 128:4096 + kc * TT + (c + 1) * 128]
                                rh = ap(sb[buf], kc * 512, [[8192, 128], [1, 512]])
                                rl = ap(sb[buf], 4096 + kc * 512, [[8192, 128], [1, 512]])
                                for (l, r) in ((lh, rh), (lh, rl), (ll, rh)):
                                    items.append((ps[bank][:, :], l, r, k == 0, k == 3 * KC - 1))
                                    k += 1
                            mm(items, reads=["hhp0", "hhp1"] + q, writes=[f"ps{bank}"])
                            v_op("tensor_scalar",
                                 aap(V_OFF + c * 1024 + cb * 512, [[16384, 128], [1, 512]]),
                                 ps[bank][:, :], sb["rcol"][:, c:c + 1], None, ALU.mult,
                                 reads=["rcol"], writes=[dst_res, f"ps{bank}"])

        def emit_store(tl):
            for c in range(CPT):
                for kcg in range(2):
                    bank = (c * 2 + kcg) % 4
                    items = []
                    for j in range(4):
                        kc = kcg * 4 + j
                        items.append((ps[bank][:, j * 128:(j + 1) * 128],
                                      ap(sb["x1"], kc * TT + c * 128, [[4096, 128], [1, 128]]),
                                      sb["ident"][:, :]))
                    trs(items, reads=X1K + ["ident"], writes=[f"ps{bank}"])
                    v_op("tensor_copy",
                         aap(12288 + c * 1024 + kcg * 512, [[16384, 128], [1, 512]]),
                         ps[bank][:, :], writes=["A3", f"ps{bank}"])
                dma("G", ap(y_t, (tl * TT + c * 128) * D, [[D, 128], [1, D]]),
                    aap(12288 + c * 1024, [[16384, 128], [1, 1024]]),
                    reads=["A3"])

        emit_loads(0)
        emit_intr_xg(0)
        stats_rms(sb["x0"], X0K, 4, 5, "rrep2")
        emit_qkv()

        for tl in range(NT):
            for c in range(CPT):
                for hg in range(2):
                    bank = (c * 2 + hg) % 4
                    items = []
                    for j in range(4):
                        h = hg * 4 + j
                        items.append((ps[bank][:, j * 128:(j + 1) * 128],
                                      aap(K_OFF + h * TT + c * 128, [[16384, 128], [1, 128]]),
                                      sb["ident"][:, :]))
                    trs(items, reads=["A1", "ident"], writes=[f"ps{bank}"])
                    v_op("tensor_copy",
                         ap(sb["stage"], c * 1024 + hg * 512, [[4096, 128], [1, 512]]),
                         ps[bank][:, :], writes=[f"st{c}", f"ps{bank}"])

            for i in range(CPT):
                for bg in range(2):
                    bank = (2 - (i % 2) * 2) + bg
                    items = []
                    for j in range(4):
                        h = bg * 4 + j
                        items.append((ps[bank][:, j * 128:(j + 1) * 128],
                                      aap(K_OFF + h * TT + i * 128, [[16384, 128], [1, 128]]),
                                      aap(Q_OFF + h * TT + i * 128, [[16384, 128], [1, 128]]),
                                      True, True))
                    mm(items, reads=["A0", "A1"], writes=[f"ps{bank}"])
                    v_op("tensor_tensor",
                         ap(sb["att"], bg * 512, [[1024, 128], [1, 512]]),
                         ps[bank][:, :],
                         ap(sb["mask"], 0, [[128, 128], [0, 4], [1, 128]]),
                         ALU.mult, reads=["mask"], writes=["att", f"ps{bank}"])
                for bg in range(2):
                    bank = 4 + bg
                    items = []
                    for j in range(4):
                        h = bg * 4 + j
                        s = slot_of(h)
                        items.append((ps[bank][:, j * 128:(j + 1) * 128],
                                      ap(sb["att"], h * 128, [[1024, 128], [1, 128]]),
                                      aap(V_OFF + i * 1024 + h * 128, [[16384, 128], [1, 128]]),
                                      True, False))
                        items.append((ps[bank][:, j * 128:(j + 1) * 128],
                                      aap(Q_OFF + h * TT + i * 128, [[16384, 128], [1, 128]]),
                                      sb["Ssb"][:, s * 128:(s + 1) * 128],
                                      False, True))
                    mm(items, reads=["att", "A0", "A2", "Sb6", "Sb7"],
                       writes=[f"ps{bank}"])
                for bg in range(2):
                    sbank = 6 + bg
                    sitems = []
                    for j in range(4):
                        h = 2 * j + bg
                        sl = slot_of(h) % 4
                        sitems.append((ps[sbank][:, sl * 128:(sl + 1) * 128],
                                       ap(sb["stage"], i * 1024 + h * 128, [[4096, 128], [1, 128]]),
                                       aap(V_OFF + i * 1024 + h * 128, [[16384, 128], [1, 128]]),
                                       True, True))
                    mm(sitems, reads=[f"st{i}", "A2"], writes=[f"ps{sbank}"])
                    sres = "Sb6" if bg == 0 else "Sb7"
                    v_op("tensor_add", sb["Ssb"][:, bg * 512:(bg + 1) * 512],
                         ps[sbank][:, 0:512], sb["Ssb"][:, bg * 512:(bg + 1) * 512],
                         writes=[sres, f"ps{sbank}"])
                for bg in range(2):
                    bank = 4 + bg
                    tsl = "ta0" if bg == 0 else "ta1"
                    s_act(sb[tsl][:, :], ps[bank][:, :], AF.Square,
                          writes=[tsl, f"ps{bank}"])
                    v_op("reduce_sum",
                         ap(sb["ro"], bg * 4, [[8, 128], [1, 4]]),
                         ap(sb[tsl], 0, [[512, 128], [128, 4], [1, 128]]),
                         AX.X, reads=[tsl], writes=["ro"])
                s_act(sb["ro2"][:, :], sb["ro"][:, :], AF.Ln,
                      bias=sb["ceps"][:, 0:1], scale=sb["cDV"][:, 0:1],
                      reads=["ro", "ceps", "cDV"], writes=["ro2"])
                s_act(sb["ro"][:, :], sb["ro2"][:, :], AF.Exp,
                      scale=sb["cNH"][:, 0:1], reads=["ro2", "cNH"], writes=["ro"])
                for bg in range(2):
                    bank = 4 + bg
                    v_op("tensor_tensor",
                         ap(sb["on_"], bg * 512, [[1024, 128], [1, 512]]),
                         ps[bank][:, :],
                         ap(sb["ro"], bg * 4, [[8, 128], [1, 4], [0, 128]]),
                         ALU.mult, reads=["ro"], writes=["on_", f"ps{bank}"])
                for bg in range(2):
                    bank = (2 - (i % 2) * 2) + bg
                    items = []
                    for j in range(4):
                        h = bg * 4 + j
                        items.append((ps[bank][:, j * 128:(j + 1) * 128],
                                      ap(sb["on_"], h * 128, [[1024, 128], [1, 128]]),
                                      sb["ident"][:, :]))
                    trs(items, reads=["on_", "ident"], writes=[f"ps{bank}"])
                    spl = "sp0" if bg == 0 else "sp1"
                    s_act(sb[spl][:, :], ps[bank][:, :], AF.Copy,
                          scale=sb["gocol"][:, 0:1],
                          reads=["gocol"], writes=[spl, f"ps{bank}"])
                    onh = ab16(2 * ONT_H_OFF + bg * 4 * 512 + i * 128,
                               [[32768, 128], [512, 4], [1, 128]])
                    onl = ab16(2 * ONT_L_OFF + bg * 4 * 512 + i * 128,
                               [[32768, 128], [512, 4], [1, 128]])
                    s_act(onh, ap(sb[spl], 0, [[512, 128], [128, 4], [1, 128]]),
                          AF.Copy, reads=[spl], writes=["A3"])
                    v_op("tensor_sub", onl,
                         ap(sb[spl], 0, [[512, 128], [128, 4], [1, 128]]), onh,
                         reads=[spl], writes=["A3"])

            if tl + 1 < NT:
                emit_loads(tl + 1)

            for cb in range(2):
                buf = "wa" if cb % 2 == 0 else "wb"
                q = WAQ if buf == "wa" else WBQ
                load_hl("wo", cb * 512, 512, KC, D, buf)
                for oc_in in range(4):
                    bank = oc_in % 2
                    items = []
                    k = 0
                    for kc in range(KC):
                        wh = ap(sb[buf], kc * 512 + oc_in * 128, [[8192, 128], [1, 128]])
                        wl = ap(sb[buf], 4096 + kc * 512 + oc_in * 128, [[8192, 128], [1, 128]])
                        oh = ab16(2 * ONT_H_OFF + kc * 512, [[32768, 128], [1, 512]])
                        ol = ab16(2 * ONT_L_OFF + kc * 512, [[32768, 128], [1, 512]])
                        for (l, r) in ((wh, oh), (wh, ol), (wl, oh)):
                            items.append((ps[bank][:, :], l, r, k == 0, k == 3 * KC - 1))
                            k += 1
                    mm(items, reads=["A3"] + q, writes=[f"ps{bank}"])
                    oc = cb * 4 + oc_in
                    v_op("tensor_add", sb["x1"][:, oc * TT:(oc + 1) * TT],
                         ps[bank][:, :], sb["x0"][:, oc * TT:(oc + 1) * TT],
                         reads=[f"x0k{oc}"],
                         writes=[f"x1k{oc}", "x1spl", f"ps{bank}"])
                    spl = XG_SLOTS[oc % 4]
                    v_op("tensor_scalar", sb[spl][:, :],
                         sb["x1"][:, oc * TT:(oc + 1) * TT], sb["g2c"][:, oc:oc + 1],
                         None, ALU.mult, reads=[f"x1k{oc}", "g2c"], writes=[spl])
                    s_act(sb["hhb"][:, oc * TT:(oc + 1) * TT], sb[spl][:, :],
                          AF.Copy, reads=[spl], writes=["hhp0", "hhp1"])
                    v_op("tensor_sub", sb["hhb"][:, 4096 + oc * TT:4096 + (oc + 1) * TT],
                         sb[spl][:, :], sb["hhb"][:, oc * TT:(oc + 1) * TT],
                         reads=[spl], writes=["hhp0", "hhp1"])

            stats_rms(sb["x1"], X1K, 4, 5)

            for cb in range(16):
                goff = 0 if cb % 2 == 0 else 2048
                gq = [WAQ[0], WAQ[1]] if cb % 2 == 0 else [WAQ[2], WAQ[3]]
                uq = [WBQ[0], WBQ[1]] if cb % 2 == 0 else [WBQ[2], WBQ[3]]
                dma("Y", ap(sb["wa"], goff * 2, [[8192, 128], [256, KC], [1, 256]]),
                    ap(dwh["wg"], cb * 256, [[INTER, 128], [128 * INTER, KC], [1, 256]]),
                    reads=["Dwgh"], writes=[gq[0]])
                dma("Y", ap(sb["wa"], goff * 2 + 2048, [[8192, 128], [256, KC], [1, 256]]),
                    ap(dwl["wg"], cb * 256, [[INTER, 128], [128 * INTER, KC], [1, 256]]),
                    reads=["Dwgl"], writes=[gq[1]])
                dma("Y", ap(sb["wb"], goff * 2, [[8192, 128], [256, KC], [1, 256]]),
                    ap(dwh["wu"], cb * 256, [[INTER, 128], [128 * INTER, KC], [1, 256]]),
                    reads=["Dwuh"], writes=[uq[0]])
                dma("Y", ap(sb["wb"], goff * 2 + 2048, [[8192, 128], [256, KC], [1, 256]]),
                    ap(dwl["wu"], cb * 256, [[INTER, 128], [128 * INTER, KC], [1, 256]]),
                    reads=["Dwul"], writes=[uq[1]])
                for oc_in in range(2):
                    oc = cb * 2 + oc_in
                    gb = oc_in * 2
                    ub = oc_in * 2 + 1
                    items = []
                    k = 0
                    for kc in range(KC):
                        wh = ap(sb["wa"], goff * 2 + kc * 256 + oc_in * 128, [[8192, 128], [1, 128]])
                        wl = ap(sb["wa"], goff * 2 + 2048 + kc * 256 + oc_in * 128, [[8192, 128], [1, 128]])
                        for (l, r) in ((wh, hh_h(kc)), (wh, hh_l(kc)), (wl, hh_h(kc))):
                            items.append((ps[gb][:, :], l, r, k == 0, k == 3 * KC - 1))
                            k += 1
                    mm(items, reads=["hhp0", "hhp1"] + gq, writes=[f"ps{gb}"])
                    sgs = "sg0" if oc % 2 == 0 else "sg1"
                    sgo = 0 if oc % 2 == 0 else 512
                    spl = "sp0" if oc % 2 == 0 else "sp1"
                    v_op("tensor_tensor", sb[spl][:, :], ps[gb][:, :],
                         sb["rrep"][:, :], ALU.mult,
                         reads=["rrep"], writes=[spl, f"ps{gb}"])
                    s_act(ap(sb["sg"], sgo, [[1024, 128], [1, 512]]), sb[spl][:, :],
                          AF.Silu, reads=[spl], writes=[sgs])
                    items = []
                    k = 0
                    for kc in range(KC):
                        wh = ap(sb["wb"], goff * 2 + kc * 256 + oc_in * 128, [[8192, 128], [1, 128]])
                        wl = ap(sb["wb"], goff * 2 + 2048 + kc * 256 + oc_in * 128, [[8192, 128], [1, 128]])
                        for (l, r) in ((wh, hh_h(kc)), (wh, hh_l(kc)), (wl, hh_h(kc))):
                            items.append((ps[ub][:, :], l, r, k == 0, k == 3 * KC - 1))
                            k += 1
                    mm(items, reads=["hhp0", "hhp1"] + uq, writes=[f"ps{ub}"])
                    tsl = "ta0" if oc % 2 == 0 else "ta1"
                    v_op("tensor_tensor", sb[tsl][:, :], ps[ub][:, :],
                         sb["rrep"][:, :], ALU.mult,
                         reads=["rrep"], writes=[tsl, f"ps{ub}"])
                    v_op("tensor_tensor", sb[spl][:, :],
                         ap(sb["sg"], sgo, [[1024, 128], [1, 512]]), sb[tsl][:, :],
                         ALU.mult, reads=[sgs, tsl], writes=[spl])
                    mres = f"A{oc // 16}"
                    lres = f"A{2 + oc // 16}"
                    mh = ab16(oc * 512, [[32768, 128], [1, 512]])
                    ml = ab16(16384 + oc * 512, [[32768, 128], [1, 512]])
                    s_act(mh, sb[spl][:, :], AF.Copy, reads=[spl], writes=[mres])
                    v_op("tensor_sub", ml, sb[spl][:, :], mh,
                         reads=[spl, mres], writes=[lres])

            for oc2 in range(8):
                buf = "wa" if oc2 % 2 == 0 else "wb"
                q = WAQ if buf == "wa" else WBQ
                dma("Y", ap(sb[buf], 0, [[8192, 128], [128, IC], [1, 128]]),
                    ap(dwh["wd"], oc2 * 128, [[D, 128], [128 * D, IC], [1, 128]]),
                    reads=["Dwdh"], writes=q[0:2])
                dma("Y", ap(sb[buf], 4096, [[8192, 128], [128, IC], [1, 128]]),
                    ap(dwl["wd"], oc2 * 128, [[D, 128], [128 * D, IC], [1, 128]]),
                    reads=["Dwdl"], writes=q[2:4])
                bank = 4 + oc2 % 2
                items = []
                k = 0
                for mc in range(IC):
                    wh = ap(sb[buf], mc * 128, [[8192, 128], [1, 128]])
                    wl = ap(sb[buf], 4096 + mc * 128, [[8192, 128], [1, 128]])
                    mh = ab16(mc * 512, [[32768, 128], [1, 512]])
                    ml = ab16(16384 + mc * 512, [[32768, 128], [1, 512]])
                    for (l, r) in ((wh, mh), (wh, ml), (wl, mh)):
                        items.append((ps[bank][:, :], l, r, k == 0, k == 3 * IC - 1))
                        k += 1
                mm(items, reads=["A0", "A1", "A2", "A3"] + q,
                   writes=[f"ps{bank}"])
                tsl = "ta0" if oc2 % 2 == 0 else "ta1"
                v_op("tensor_add", sb[tsl][:, :], ps[bank][:, :],
                     sb["x1"][:, oc2 * TT:(oc2 + 1) * TT],
                     reads=[f"x1k{oc2}"], writes=[tsl, f"ps{bank}"])
                v_op("tensor_add", sb["x0"][:, oc2 * TT:(oc2 + 1) * TT],
                     sb[tsl][:, :], sb["x0"][:, oc2 * TT:(oc2 + 1) * TT],
                     reads=[tsl], writes=[f"x0k{oc2}"])

            dma("Y", wf32("wa", 0, [[256, KC], [1, 256]]),
                ap(u1, 0, [[RANK, 128], [128 * RANK, KC], [1, 256]]),
                writes=["waq0", "waq1"])
            for oc in range(2):
                bank = 6 + oc
                items = []
                for kc in range(KC):
                    items.append((ps[bank][:, :],
                                  wf32("wa", kc * 256 + oc * 128, [[1, 128]]),
                                  sb["x0"][:, kc * TT:(kc + 1) * TT],
                                  kc == 0, kc == KC - 1))
                mm(items, reads=X0K + ["waq0", "waq1"], writes=[f"ps{bank}"])

            stats_ln(sb["x0"], X0K)

            for oc in range(2):
                bank = 6 + oc
                spl = "sp0" if oc == 0 else "sp1"
                tsl = "ta0" if oc == 0 else "ta1"
                v_op("tensor_tensor", sb[spl][:, :], ps[bank][:, :],
                     sb["rrep"][:, :], ALU.mult,
                     reads=["rrep"], writes=[spl, f"ps{bank}"])
                v_op("scalar_tensor_tensor", sb[tsl][:, :],
                     sb["mrep"][:, :], sb["cu1c"][:, oc:oc + 1], sb[spl][:, :],
                     ALU.mult, ALU.add,
                     reads=["mrep", "cu1c", spl], writes=[tsl])
                s_act(sb["Pb"][:, oc * 512:(oc + 1) * 512], sb[tsl][:, :],
                      AF.Copy, reads=[tsl], writes=["Pb"])
                v_op("tensor_sub", sb["Pb"][:, 1024 + oc * 512:1024 + (oc + 1) * 512],
                     sb[tsl][:, :], sb["Pb"][:, oc * 512:(oc + 1) * 512],
                     reads=[tsl], writes=["Pb"])

            apply_ln(sb["x0"], X0K,
                     lambda kc: aap(X2_OFF + kc * TT, [[16384, 128], [1, TT]]),
                     ["A0"])

            if tl + 1 < NT:
                emit_intr_xg(tl + 1)

            pending_q = []

            def flush_q():
                (qitems, qreads) = pending_q.pop(0)
                mm(qitems, reads=qreads, writes=["ps2", "ps3"])

            for vb in range(4):
                wqa = ["waq0", "waq1"] if vb % 2 == 0 else ["waq2", "waq3"]
                wqb = ["wbq0", "wbq1"] if vb % 2 == 0 else ["wbq2", "wbq3"]
                wao = (vb % 2) * 4096
                dma("Y", ap(sb["wa"], wao, [[8192, 128], [1024, 2], [1, 1024]]),
                    ap(dwh["v1"], vb * 1024, [[INTER, 128], [128 * INTER, 2], [1, 1024]]),
                    reads=["Dv1h"], writes=[wqa[0]])
                dma("Y", ap(sb["wa"], wao + 2048, [[8192, 128], [1024, 2], [1, 1024]]),
                    ap(dwl["v1"], vb * 1024, [[INTER, 128], [128 * INTER, 2], [1, 1024]]),
                    reads=["Dv1l"], writes=[wqa[1]])
                dma("Y", ap(sb["wb"], wao, [[8192, 128], [256, 8], [1, 256]]),
                    ap(dwh["u2"], vb * 1024 * RANK, [[RANK, 128], [128 * RANK, 8], [1, 256]]),
                    reads=["Du2h"], writes=[wqb[0]])
                dma("Y", ap(sb["wb"], wao + 2048, [[8192, 128], [256, 8], [1, 256]]),
                    ap(dwl["u2"], vb * 1024 * RANK, [[RANK, 128], [128 * RANK, 8], [1, 256]]),
                    reads=["Du2l"], writes=[wqb[1]])
                for ocl in range(8):
                    oc = vb * 8 + ocl
                    bank = ocl % 2
                    items = []
                    k = 0
                    for rc in range(2):
                        wh = ap(sb["wa"], wao + rc * 1024 + ocl * 128, [[8192, 128], [1, 128]])
                        wl = ap(sb["wa"], wao + 2048 + rc * 1024 + ocl * 128, [[8192, 128], [1, 128]])
                        ph = sb["Pb"][:, rc * 512:(rc + 1) * 512]
                        pl = sb["Pb"][:, 1024 + rc * 512:1024 + (rc + 1) * 512]
                        for (l, r) in ((wh, ph), (wh, pl), (wl, ph)):
                            items.append((ps[bank][:, :], l, r, k == 0, k == 5))
                            k += 1
                    mm(items, reads=["Pb"] + wqa, writes=[f"ps{bank}"])
                    if len(pending_q) >= 2:
                        flush_q()
                    spl = "sp0" if oc % 2 == 0 else "sp1"
                    s_act(sb[spl][:, :], ps[bank][:, :],
                          AF.Gelu_apprx_tanh, bias=sb["b1c"][:, oc:oc + 1],
                          reads=["b1c"], writes=[spl, f"ps{bank}"])
                    hsl = f"Hb{oc % 3}"
                    hso = (oc % 3) * 512
                    s_act(sb["Hb"][:, hso:hso + 512], sb[spl][:, :],
                          AF.Copy, reads=[spl], writes=[hsl])
                    v_op("tensor_sub", sb["Hb"][:, 1536 + hso:1536 + hso + 512],
                         sb[spl][:, :], sb["Hb"][:, hso:hso + 512],
                         reads=[spl], writes=[hsl])
                    items = []
                    for rc2 in range(2):
                        wh = ap(sb["wb"], wao + ocl * 256 + rc2 * 128, [[8192, 128], [1, 128]])
                        wl = ap(sb["wb"], wao + 2048 + ocl * 256 + rc2 * 128, [[8192, 128], [1, 128]])
                        hhp = sb["Hb"][:, hso:hso + 512]
                        hlp = sb["Hb"][:, 1536 + hso:1536 + hso + 512]
                        k = 0
                        for (l, r) in ((wh, hhp), (wh, hlp), (wl, hhp)):
                            items.append((ps[2 + rc2][:, :], l, r,
                                          oc == 0 and k == 0, oc == IC - 1 and k == 2))
                            k += 1
                    pending_q.append((items, [hsl] + wqb))
            while pending_q:
                flush_q()
            v_op("tensor_copy", aap(QS_OFF, [[16384, 128], [1, 512]]),
                 ps[2][:, :], writes=["A1", "ps2"])
            v_op("tensor_copy", aap(QS_OFF + 512, [[16384, 128], [1, 512]]),
                 ps[3][:, :], writes=["A1", "ps3"])

            dma("Y", wf32("wb", 0, [[1024, 2], [1, 1024]]),
                ap(v2, 0, [[D, 128], [128 * D, 2], [1, 1024]]),
                writes=["wbq0", "wbq1"])
            for oc2 in range(8):
                bank = 4 + oc2 % 2
                items = []
                for rc in range(2):
                    items.append((ps[bank][:, :],
                                  wf32("wb", rc * 1024 + oc2 * 128, [[1, 128]]),
                                  aap(QS_OFF + rc * 512, [[16384, 128], [1, 512]]),
                                  rc == 0, rc == 1))
                mm(items, reads=["A1", "wbq0", "wbq1"], writes=[f"ps{bank}"])
                v_op("scalar_tensor_tensor", sb["x1"][:, oc2 * TT:(oc2 + 1) * TT],
                     ps[bank][:, :], sb["b2c"][:, oc2:oc2 + 1],
                     aap(X2_OFF + oc2 * TT, [[16384, 128], [1, TT]]),
                     ALU.add, ALU.add,
                     reads=["b2c", "A0"], writes=[f"x1k{oc2}", f"ps{bank}"])

            stats_ln(sb["x1"], X1K)
            if tl + 1 < NT:
                stats_rms(sb["x0"], X0K, 4, 5, "rrep2")
                emit_qkv()
            apply_ln(sb["x1"], X1K,
                     lambda kc: sb["x1"][:, kc * TT:(kc + 1) * TT],
                     X1K)
            emit_store(tl)

        for s, v in sorted(E.cnt.items()):
            if s.startswith("qr"):
                E.ops["G"].append(("w", s, v))

        sems = {}
        for s in sorted(set(list(E.cnt.keys()) + ["sT", "sV", "sS", "sG"])):
            sems[s] = e(nc.semaphore(s))
        block = e(nc.Block())

        def runner(key):
            def run(ee):
                for item in E.ops[key]:
                    if item[0] == "w":
                        ee.wait_ge(sems[item[1]], item[2])
                    else:
                        _, fn, sem, inc = item
                        fn(ee).then_inc(sems[sem], inc)
            return run

        block.tensor(runner("T"))
        block.vector(runner("V"))
        block.scalar(runner("S"))
        block.gpsimd(runner("G"))
        block.sync(runner("Y"))

    return nc


def kernel(**inputs):
    x = np.ascontiguousarray(np.asarray(inputs["L0_attn_hidden_states"], dtype=np.float32))
    wmap = {
        "wq": "L0_attn_Wq", "wk": "L0_attn_Wk", "wv": "L0_attn_Wv",
        "wo": "L0_attn_Wo", "wg": "L0_attn_Wgate", "wu": "L0_attn_Wup",
        "wd": "L0_attn_Wdown", "u1": "L0_ffn_U1", "v1": "L0_ffn_V1",
        "b1": "L0_ffn_b1", "u2": "L0_ffn_U2", "v2": "L0_ffn_V2",
        "b2": "L0_ffn_b2", "g1": "L0_attn_attn_norm_w",
        "g2": "L0_attn_mlp_norm_w", "go": "L0_attn_attn_out_norm_w",
    }
    ws = {k: np.ascontiguousarray(np.asarray(inputs[v], dtype=np.float32))
          for k, v in wmap.items()}
    nc = build_nc()
    in_maps = []
    for c in range(8):
        b, p = c // 4, c % 4
        xin = np.zeros((TSEQ, D), dtype=np.float32)
        if p > 0:
            xin[PRE - p * TCORE:PRE] = x[b, :p * TCORE]
        xin[PRE:] = x[b, p * TCORE:(p + 1) * TCORE]
        m = {"xin": xin}
        m.update(ws)
        in_maps.append(m)
    res = run_bass_kernel_spmd(nc, in_maps, core_ids=list(range(8)))
    y = np.empty((B, TSEQ, D), dtype=np.float32)
    for c in range(8):
        b, p = c // 4, c % 4
        y[b, p * TCORE:(p + 1) * TCORE] = res.results[c]["y"]
    return y


# version 37 (final)
# speedup vs baseline: 1.0064x; 1.0064x over previous
import numpy as np
import concourse.bass as bass
import concourse.mybir as mybir
from concourse.bass_utils import run_bass_kernel_spmd

F32 = mybir.dt.float32
BF16 = mybir.dt.bfloat16
AF = mybir.ActivationFunctionType
ALU = mybir.AluOpType
AX = mybir.AxisListType

B, TSEQ, D = 2, 8192, 1024
H, DK, DV = 8, 128, 128
INTER, RANK = 4096, 256
KC = D // 128
TT = 512
NT = 4
TCORE = TT * NT
CPT = TT // 128
PRE = TSEQ - TCORE
PREG = PRE // TT
EPS = 1e-5
IC = INTER // 128

Q_OFF, K_OFF, V_OFF, O_OFF = 0, 4096, 8192, 12288
X2_OFF = 0
QS_OFF = 4096
G1REP_OFF = 9216
G1ROW_OFF = 10240
ONT_H_OFF = 12288
ONT_L_OFF = 14336


def _h(g):
    return g.tensor if hasattr(g, "tensor") else g


def slot_of(h):
    return (h % 2) * 4 + (h // 2)


class Em:

    def __init__(self):
        self.ops = {k: [] for k in "TVSGY"}
        self.cnt = {}
        self.lastwait = {}
        self.writer = {}
        self.readers = {}

    def op(self, eng, fn, reads=(), writes=(), dma=False):
        need = {}
        toks = []
        for r in reads:
            t = self.writer.get(r)
            if t is not None:
                toks.append(t)
        for w in writes:
            t = self.writer.get(w)
            if t is not None:
                toks.append(t)
            toks.extend(self.readers.get(w, ()))
        for (sem, val) in toks:
            if need.get(sem, 0) < val:
                need[sem] = val
        for sem, val in need.items():
            key = (eng, sem)
            if self.lastwait.get(key, 0) >= val:
                continue
            self.lastwait[key] = val
            self.ops[eng].append(("w", sem, val))
        if dma:
            sem = "q" + (writes[0] if writes else "r" + reads[0])
        else:
            sem = "s" + eng
        inc = 16 if dma else 1
        self.cnt[sem] = self.cnt.get(sem, 0) + inc
        tok = (sem, self.cnt[sem])
        self.ops[eng].append(("i", fn, sem, inc))
        for w in writes:
            self.writer[w] = tok
            self.readers[w] = []
        for r in reads:
            self.readers.setdefault(r, []).append(tok)
        return tok


def build_nc():
    nc = bass.Bass("TRN2", target_bir_lowering=False)

    xin = nc.dram_tensor("xin", [TSEQ, D], F32, kind="ExternalInput")
    y_t = nc.dram_tensor("y", [TCORE, D], F32, kind="ExternalOutput")
    wq = nc.dram_tensor("wq", [D, D], F32, kind="ExternalInput")
    wk = nc.dram_tensor("wk", [D, D], F32, kind="ExternalInput")
    wv = nc.dram_tensor("wv", [D, D], F32, kind="ExternalInput")
    wo = nc.dram_tensor("wo", [D, D], F32, kind="ExternalInput")
    wg = nc.dram_tensor("wg", [D, INTER], F32, kind="ExternalInput")
    wu = nc.dram_tensor("wu", [D, INTER], F32, kind="ExternalInput")
    wd = nc.dram_tensor("wd", [INTER, D], F32, kind="ExternalInput")
    u1 = nc.dram_tensor("u1", [D, RANK], F32, kind="ExternalInput")
    v1 = nc.dram_tensor("v1", [RANK, INTER], F32, kind="ExternalInput")
    b1 = nc.dram_tensor("b1", [INTER], F32, kind="ExternalInput")
    u2 = nc.dram_tensor("u2", [INTER, RANK], F32, kind="ExternalInput")
    v2 = nc.dram_tensor("v2", [RANK, D], F32, kind="ExternalInput")
    b2 = nc.dram_tensor("b2", [D], F32, kind="ExternalInput")
    g1 = nc.dram_tensor("g1", [D], F32, kind="ExternalInput")
    g2 = nc.dram_tensor("g2", [D], F32, kind="ExternalInput")
    go = nc.dram_tensor("go", [DV], F32, kind="ExternalInput")

    SPLIT_W = {"wq": wq, "wk": wk, "wv": wv, "wo": wo, "wg": wg, "wu": wu,
               "wd": wd, "v1": v1, "u2": u2}
    dwh, dwl = {}, {}
    for nm, t in SPLIT_W.items():
        dwh[nm] = nc.dram_tensor("h_" + nm, list(t.shape), BF16)
        dwl[nm] = nc.dram_tensor("l_" + nm, list(t.shape), BF16)

    from contextlib import ExitStack
    with ExitStack() as ctx:
        e = ctx.enter_context
        sb = {}
        for name, shape, dt in [
            ("A", [128, 16384], F32),
            ("x0", [128, 4096], F32), ("x1", [128, 4096], F32),
            ("hhb", [128, 8192], BF16),
            ("stage", [128, 4096], F32),
            ("wa", [128, 8192], BF16), ("wb", [128, 8192], BF16),
            ("sg", [128, 1024], F32),
            ("Ssb", [128, 1024], F32),
            ("att", [128, 1024], F32), ("on_", [128, 1024], F32),
            ("Pb", [128, 2048], BF16),
            ("sp0", [128, 512], F32), ("sp1", [128, 512], F32),
            ("ro", [128, 8], F32), ("ro2", [128, 8], F32),
            ("mask", [128, 128], F32), ("ident", [128, 128], F32),
            ("ta0", [128, 512], F32), ("ta1", [128, 512], F32),
            ("tb", [128, 512], F32), ("tc", [128, 512], F32),
            ("rrep", [128, 512], F32), ("mrep", [128, 512], F32),
            ("rrep2", [128, 512], F32),
            ("onesc", [128, 1], F32), ("onesr", [1, 128], F32),
            ("negc", [128, 1], F32), ("cu1c", [128, 2], F32),
            ("rcol2", [128, 4], F32), ("mcol", [128, 4], F32),
            ("ceps", [128, 1], F32), ("cD", [128, 1], F32),
            ("cDV", [128, 1], F32), ("cNH", [128, 1], F32),
            ("g1c", [128, 8], F32), ("g2c", [128, 8], F32),
            ("b1c", [128, 32], F32), ("b2c", [128, 8], F32),
            ("gocol", [128, 1], F32),
            ("rq", [128, 4], F32), ("rcol", [128, 4], F32),
            ("Hb", [128, 3072], BF16),
        ]:
            sb[name] = e(nc.sbuf_tensor(name, shape, dt))
        ps = [e(nc.psum_tensor(f"ps{i}", [128, 512], F32)) for i in range(8)]

        A = sb["A"]

        def ap(g, off, dims):
            return bass.AP(_h(g), off, dims)

        def aap(off, dims):
            return bass.AP(_h(A), off, dims)

        def ab16(off2, dims2):
            fdims = [[s // 2, n] for (s, n) in dims2[:-1]] + [[1, dims2[-1][1] // 2]]
            return bass.AP(_h(A), off2 // 2, fdims).bitcast(BF16)

        def wf32(buf, off32, fdims32):
            bdims = ([[8192, 128]] + [[s * 2, n] for (s, n) in fdims32[:-1]]
                     + [[1, fdims32[-1][1] * 2]])
            return bass.AP(_h(sb[buf]), off32 * 2, bdims).bitcast(F32)

        rowa = ap(sb["mrep"], 0, [[512, 1], [1, 512]])
        rowb = ap(sb["rrep"], 0, [[512, 1], [1, 512]])
        rowc = ap(sb["tb"], 0, [[512, 1], [1, 512]])

        E = Em()

        def dma(eng, dst, src, reads=(), writes=()):
            return E.op(eng, lambda ee, d=dst, s=src: ee.dma_start(d, s),
                        reads=reads, writes=writes, dma=True)

        def mm(items, reads=(), writes=()):
            def fn(ee, items=tuple(items)):
                last = None
                for (o, l, r, s, p) in items:
                    last = ee.matmul(o, l, r, start=s, stop=p)
                return last
            return E.op("T", fn, reads=reads, writes=writes)

        def trs(items, reads=(), writes=()):
            def fn(ee, items=tuple(items)):
                last = None
                for (o, i, idn) in items:
                    last = ee.transpose(o, i, idn)
                return last
            return E.op("T", fn, reads=reads, writes=writes)

        def v_op(name, *args, reads=(), writes=()):
            def fn(ee, name=name, args=args):
                return getattr(ee, name)(*args)
            return E.op("V", fn, reads=reads, writes=writes)

        def s_act(out, in_, func, bias=0.0, scale=1.0, reads=(), writes=()):
            def fn(ee):
                return ee.activation(out, in_, func, bias=bias, scale=scale)
            return E.op("S", fn, reads=reads, writes=writes)

        def g_op(name, *args, reads=(), writes=()):
            def fn(ee, name=name, args=args):
                return getattr(ee, name)(*args)
            return E.op("G", fn, reads=reads, writes=writes)

        WAQ = ["waq0", "waq1", "waq2", "waq3"]
        WBQ = ["wbq0", "wbq1", "wbq2", "wbq3"]

        rawv = {"g1raw": (0, 8, g1), "g2raw": (1024, 8, g2),
                "b1raw": (2048, 32, b1), "b2raw": (3072, 8, b2)}
        for nm_, (off_, n_, t_) in rawv.items():
            dma("Y", ap(sb["stage"], off_, [[4096, n_], [1, 128]]),
                ap(t_, 0, [[128, n_], [1, 128]]), writes=[f"st{off_ // 1024}"])
        dma("Y", ap(sb["gocol"], 0, [[1, 128], [1, 1]]), ap(go, 0, [[1, 128], [1, 1]]), writes=["gocol"])
        dma("Y", aap(G1ROW_OFF, [[16384, 1], [1, 1024]]),
            ap(g1, 0, [[1024, 1], [1, 1024]]), writes=["A2"])

        ioA = ap(sb["ta0"], 0, [[512, 128], [1, 128]])
        ioB = ap(sb["tb"], 0, [[512, 128], [1, 1]])
        g_op("iota", ioA, [[1, 128]], None, 0, True, writes=["ta0"])
        g_op("iota", ioB, [[1, 1]], None, 1, True, writes=["tb"])
        g_op("memset", sb["onesc"][:, :], 1.0, writes=["onesc"])
        g_op("memset", sb["negc"][:, :], -1.0, writes=["negc"])
        g_op("memset", sb["onesr"][:, :], 1.0, writes=["onesr"])
        g_op("memset", sb["ceps"][:, :], EPS, writes=["ceps"])
        g_op("memset", sb["cD"][:, :], 1.0 / D, writes=["cD"])
        g_op("memset", sb["cDV"][:, :], 1.0 / DV, writes=["cDV"])
        g_op("memset", sb["cNH"][:, :], -0.5, writes=["cNH"])
        v_op("memset", aap(0, [[16384, 128], [1, 8192]]), 0.0,
             writes=["A0", "A1"])

        v_op("tensor_scalar", sb["ident"][:, :], ioA, ioB,
             None, ALU.is_equal, reads=["ta0", "tb"], writes=["ident"])
        v_op("tensor_scalar", sb["mask"][:, :], ioA, ioB,
             None, ALU.is_ge, reads=["ta0", "tb"], writes=["mask"])

        for (raw, dstn, n, bank) in [("g1raw", "g1c", 8, 0), ("g2raw", "g2c", 8, 1),
                                     ("b1raw", "b1c", 32, 2), ("b2raw", "b2c", 8, 3)]:
            off_, n_, _ = rawv[raw]
            trs([(ps[bank][0:128, 0:n], ap(sb["stage"], off_, [[4096, n_], [1, 128]]),
                  sb["ident"][0:n, 0:n])],
                reads=[f"st{off_ // 1024}", "ident"], writes=[f"ps{bank}"])
            v_op("tensor_copy", sb[dstn][:, :], ps[bank][0:128, 0:n],
                 writes=[dstn, f"ps{bank}"])

        for half in range(2):
            mm([(ps[4][:, :], sb["onesr"][:, :],
                 aap(G1ROW_OFF + half * 512, [[16384, 1], [1, 512]]), True, True)],
               reads=["onesr", "A2"], writes=["ps4"])
            v_op("tensor_copy", aap(G1REP_OFF + half * 512, [[16384, 128], [1, 512]]),
                 ps[4][:, :], writes=["A2", "ps4"])

        split_blocks = []
        for nm, t in SPLIT_W.items():
            tot = t.shape[0] * t.shape[1]
            nblk = tot // (128 * 4096)
            for bkt in range(nblk):
                split_blocks.append((nm, bkt))

        spc = [0]

        def emit_split_block(nm, bkt):
            off = bkt * 128 * 4096
            w_t = SPLIT_W[nm]
            if spc[0] % 2 == 0:
                wsrc = wf32("wa", 0, [[1, 4096]])
                inq = WAQ
            else:
                wsrc = ap(sb["x1"], 0, [[4096, 128], [1, 4096]])
                inq = ["x1spl"]
            spc[0] += 1
            dma("Y", wsrc, ap(w_t, off, [[4096, 128], [1, 4096]]), writes=inq)
            s_act(sb["wb"][:, 0:4096], wsrc,
                  AF.Copy, reads=inq, writes=["wbq0", "wbq1"])
            g_op("tensor_sub", sb["wb"][:, 4096:8192],
                 wsrc, sb["wb"][:, 0:4096],
                 reads=inq + ["wbq0", "wbq1"], writes=["wbq2", "wbq3"])
            dma("S", ap(dwh[nm], off, [[4096, 128], [1, 4096]]),
                sb["wb"][:, 0:4096], reads=["wbq0", "wbq1"], writes=[f"D{nm}h"])
            dma("S", ap(dwl[nm], off, [[4096, 128], [1, 4096]]),
                sb["wb"][:, 4096:8192], reads=["wbq2", "wbq3"], writes=[f"D{nm}l"])

        def load_hl(nm, c0, C, nkc, ncols_total, buf):
            q = WAQ if buf == "wa" else WBQ
            dma("Y", ap(sb[buf], 0, [[8192, 128], [C, nkc], [1, C]]),
                ap(dwh[nm], c0, [[ncols_total, 128], [128 * ncols_total, nkc], [1, C]]),
                reads=[f"D{nm}h"], writes=q[0:2])
            dma("Y", ap(sb[buf], 4096, [[8192, 128], [C, nkc], [1, C]]),
                ap(dwl[nm], c0, [[ncols_total, 128], [128 * ncols_total, nkc], [1, C]]),
                reads=[f"D{nm}l"], writes=q[2:4])

        def tri(out_ap, wh, wl, xh, xl):
            return [(out_ap, wh, xh), (out_ap, wh, xl), (out_ap, wl, xh)]

        def stats_rms(src, srcres, rowbank, bcbank, dst="rrep"):
            if isinstance(srcres, str):
                srcres = [srcres]
            for kc in range(KC):
                tsl = "ta0" if kc % 2 == 0 else "ta1"
                s_act(sb[tsl][:, :], src[:, kc * TT:(kc + 1) * TT], AF.Square,
                      reads=[srcres[kc % len(srcres)]], writes=[tsl])
                if kc == 0:
                    v_op("tensor_copy", sb["tb"][:, :], sb[tsl][:, :],
                         reads=[tsl], writes=["tb"])
                else:
                    v_op("tensor_add", sb["tb"][:, :], sb["tb"][:, :], sb[tsl][:, :],
                         reads=[tsl], writes=["tb"])
            mm([(ps[rowbank][0:1, :], sb["onesc"][:, :], sb["tb"][:, :], True, True)],
               reads=["onesc", "tb"], writes=[f"ps{rowbank}"])
            rowt = ap(sb["tc"], 0, [[512, 1], [1, 512]])
            s_act(rowt, ps[rowbank][0:1, :], AF.Ln,
                  bias=sb["ceps"][0:1, 0:1], scale=sb["cD"][0:1, 0:1],
                  reads=["ceps", "cD"], writes=["tc", f"ps{rowbank}"])
            rowd = ap(sb[dst], 0, [[512, 1], [1, 512]])
            s_act(rowd, rowt, AF.Exp, scale=sb["cNH"][0:1, 0:1],
                  reads=["tc", "cNH"], writes=[dst])
            mm([(ps[bcbank][:, :], sb["onesr"][:, :], rowd, True, True)],
               reads=["onesr", dst], writes=[f"ps{bcbank}"])
            items = []
            for c in range(CPT):
                items.append((ps[rowbank][:, c:c + 1],
                              ap(sb[dst], c * 128, [[512, 1], [1, 128]]),
                              sb["onesr"][0:1, 0:1], True, True))
            mm(items, reads=["onesr", dst], writes=[f"ps{rowbank}"])
            v_op("tensor_copy", sb["rcol"][:, :], ps[rowbank][:, 0:CPT],
                 writes=["rcol", f"ps{rowbank}"])
            v_op("tensor_copy", sb[dst][:, :], ps[bcbank][:, :],
                 writes=[dst, f"ps{bcbank}"])

        def stats_ln(src, srcres, bcast=True):
            if isinstance(srcres, str):
                srcres = [srcres]
            for kc in range(KC):
                if kc == 0:
                    v_op("tensor_copy", sb["tc"][:, :], src[:, 0:TT],
                         reads=[srcres[kc % len(srcres)]], writes=["tc"])
                else:
                    v_op("tensor_add", sb["tc"][:, :], sb["tc"][:, :],
                         src[:, kc * TT:(kc + 1) * TT],
                         reads=[srcres[kc % len(srcres)]], writes=["tc"])
            for kc in range(KC):
                tsl = "ta0" if kc % 2 == 0 else "ta1"
                s_act(sb[tsl][:, :], src[:, kc * TT:(kc + 1) * TT], AF.Square,
                      reads=[srcres[kc % len(srcres)]], writes=[tsl])
                if kc == 0:
                    v_op("tensor_copy", sb["tb"][:, :], sb[tsl][:, :],
                         reads=[tsl], writes=["tb"])
                else:
                    v_op("tensor_add", sb["tb"][:, :], sb["tb"][:, :], sb[tsl][:, :],
                         reads=[tsl], writes=["tb"])
            mm([(ps[0][0:1, :], sb["onesc"][:, :], sb["tc"][:, :], True, True)],
               reads=["onesc", "tc"], writes=["ps0"])
            mm([(ps[1][0:1, :], sb["onesc"][:, :], sb["tb"][:, :], True, True)],
               reads=["onesc", "tb"], writes=["ps1"])
            s_act(rowa, ps[0][0:1, :], AF.Copy, scale=sb["cD"][0:1, 0:1],
                  reads=["cD"], writes=["mrep", "ps0"])
            s_act(rowb, ps[1][0:1, :], AF.Copy, scale=sb["cD"][0:1, 0:1],
                  reads=["cD"], writes=["rrep", "ps1"])
            v_op("tensor_mul", rowc, rowa, rowa, reads=["mrep"], writes=["tb"])
            v_op("tensor_sub", rowb, rowb, rowc, reads=["tb"], writes=["rrep"])
            s_act(rowb, rowb, AF.Ln, bias=sb["ceps"][0:1, 0:1],
                  reads=["ceps"], writes=["rrep"])
            s_act(rowb, rowb, AF.Exp, scale=sb["cNH"][0:1, 0:1],
                  reads=["cNH"], writes=["rrep"])
            v_op("tensor_mul", rowc, rowa, rowb,
                 reads=["mrep", "rrep"], writes=["tb"])
            if not bcast:
                items = []
                for c in range(CPT):
                    items.append((ps[6][:, c:c + 1],
                                  ap(sb["rrep"], c * 128, [[512, 1], [1, 128]]),
                                  sb["onesr"][0:1, 0:1], True, True))
                mm(items, reads=["onesr", "rrep"], writes=["ps6"])
                items = []
                for c in range(CPT):
                    items.append((ps[7][:, c:c + 1],
                                  ap(sb["tb"], c * 128, [[512, 1], [1, 128]]),
                                  sb["onesr"][0:1, 0:1], True, True))
                mm(items, reads=["onesr", "tb"], writes=["ps7"])
                v_op("tensor_copy", sb["rcol2"][:, :], ps[6][:, 0:CPT],
                     writes=["rcol2", "ps6"])
                v_op("tensor_scalar", sb["mcol"][:, :], ps[7][:, 0:CPT],
                     sb["negc"][:, 0:1], None, ALU.mult,
                     reads=["negc"], writes=["mcol", "ps7"])
                return
            mm([(ps[2][:, :], sb["onesr"][:, :], rowb, True, True)],
               reads=["onesr", "rrep"], writes=["ps2"])
            mm([(ps[3][:, :], sb["onesr"][:, :], rowc, True, True)],
               reads=["onesr", "tb"], writes=["ps3"])
            v_op("tensor_copy", sb["rrep"][:, :], ps[2][:, :], writes=["rrep", "ps2"])
            v_op("tensor_copy", sb["mrep"][:, :], ps[3][:, :], writes=["mrep", "ps3"])

        def apply_ln(src, srcres, dst_ap_fn, dstres):
            if isinstance(srcres, str):
                srcres = [srcres]
            for kc in range(KC):
                tsl = "ta0" if kc % 2 == 0 else "ta1"
                v_op("tensor_mul", sb[tsl][:, :], src[:, kc * TT:(kc + 1) * TT],
                     sb["rrep"][:, :],
                     reads=[srcres[kc % len(srcres)], "rrep"], writes=[tsl])
                v_op("tensor_sub", dst_ap_fn(kc), sb[tsl][:, :],
                     sb["mrep"][:, :], reads=[tsl, "mrep"], writes=dstres)

        dma("Y", wf32("wa", 0, [[256, KC], [1, 256]]),
            ap(u1, 0, [[RANK, 128], [128 * RANK, KC], [1, 256]]),
            writes=["waq0", "waq1"])
        items = []
        for kc in range(KC):
            items.append((ps[6][0:1, 0:256], sb["negc"][:, :],
                          wf32("wa", kc * 256, [[1, 256]]),
                          kc == 0, kc == KC - 1))
        mm(items, reads=["negc", "waq0", "waq1"], writes=["ps6"])
        v_op("tensor_copy", ap(sb["sp0"], 0, [[512, 1], [1, 256]]),
             ps[6][0:1, 0:256], writes=["sp0", "ps6"])
        for j in range(2):
            trs([(ps[5 + 2 * j][:, 0:1],
                  ap(sb["sp0"], j * 128, [[512, 1], [1, 128]]),
                  sb["ident"][0:1, 0:1])],
                reads=["sp0", "ident"], writes=[f"ps{5 + 2 * j}"])
            v_op("tensor_copy", sb["cu1c"][:, j:j + 1], ps[5 + 2 * j][:, 0:1],
                 writes=["cu1c", f"ps{5 + 2 * j}"])

        sbi = [0]

        def pre_stage(g):
            for c in range(CPT):
                dma("Y", ap(sb["stage"], c * 1024, [[4096, 128], [1, 1024]]),
                    ap(xin, (g * TT + c * 128) * D, [[D, 128], [1, D]]),
                    writes=[f"st{c}"])

        pre_stage(0)
        for g in range(PREG):
            for pair in range(2):
                for cc in range(2):
                    c = pair * 2 + cc
                    s_act(ap(sb["x0"], c * 1024, [[4096, 128], [1, 1024]]),
                          ap(sb["stage"], c * 1024, [[4096, 128], [1, 1024]]),
                          AF.Square, reads=[f"st{c}"], writes=["x0"])
                v_op("reduce_sum", sb["rq"][:, 0:2],
                     ap(sb["x0"], pair * 2048, [[4096, 128], [1024, 2], [1, 1024]]),
                     AX.X, reads=["x0"], writes=["rq"])
                s_act(sb["rq"][:, 2:4], sb["rq"][:, 0:2], AF.Ln,
                      bias=sb["ceps"][:, 0:1], scale=sb["cD"][:, 0:1],
                      reads=["ceps", "cD"], writes=["rq"])
                s_act(sb["rq"][:, 0:2], sb["rq"][:, 2:4], AF.Exp,
                      scale=sb["cNH"][:, 0:1], reads=["cNH"], writes=["rq"])
                hpres = f"hhp{pair}"
                for cc in range(2):
                    c = pair * 2 + cc
                    v_op("scalar_tensor_tensor",
                         ap(sb["on_"], 0, [[1024, 128], [1, 1024]]),
                         ap(sb["stage"], c * 1024, [[4096, 128], [1, 1024]]),
                         sb["rq"][:, cc:cc + 1], aap(G1REP_OFF, [[16384, 128], [1, 1024]]),
                         ALU.mult, ALU.mult,
                         reads=[f"st{c}", "rq", "A2"], writes=["on_"])
                    s_act(sb["hhb"][:, c * 1024:(c + 1) * 1024],
                          ap(sb["on_"], 0, [[1024, 128], [1, 1024]]),
                          AF.Copy, reads=["on_"], writes=[hpres])
                    v_op("tensor_sub", sb["hhb"][:, 4096 + c * 1024:4096 + (c + 1) * 1024],
                         ap(sb["on_"], 0, [[1024, 128], [1, 1024]]),
                         sb["hhb"][:, c * 1024:(c + 1) * 1024],
                         reads=["on_"], writes=[hpres])
            if g + 1 < PREG:
                pre_stage(g + 1)
            for _ in range(3):
                if sbi[0] < len(split_blocks):
                    emit_split_block(*split_blocks[sbi[0]])
                    sbi[0] += 1
            for pair in range(2):
                hpres = f"hhp{pair}"
                for kc in range(KC):
                    b0 = 0 if kc % 2 == 0 else 2
                    for half in range(2):
                        if half == 0 and kc >= 4:
                            continue
                        bank = b0 + half
                        items = []
                        k = 0
                        for cc in range(2):
                            c = pair * 2 + cc
                            lh = sb["hhb"][:, c * 1024 + kc * 128:c * 1024 + (kc + 1) * 128]
                            ll = sb["hhb"][:, 4096 + c * 1024 + kc * 128:4096 + c * 1024 + (kc + 1) * 128]
                            rh = sb["hhb"][:, c * 1024 + half * 512:c * 1024 + half * 512 + 512]
                            rl = sb["hhb"][:, 4096 + c * 1024 + half * 512:4096 + c * 1024 + half * 512 + 512]
                            for (l, r) in ((lh, rh), (lh, rl), (ll, rh)):
                                items.append((ps[bank][:, :], l, r, k == 0, k == 5))
                                k += 1
                        mm(items, reads=[hpres], writes=[f"ps{bank}"])
                        res = "A0" if kc < 4 else "A1"
                        v_op("tensor_add",
                             aap(kc * 1024 + half * 512, [[16384, 128], [1, 512]]),
                             aap(kc * 1024 + half * 512, [[16384, 128], [1, 512]]),
                             ps[bank][:, :], writes=[res, f"ps{bank}"])

        while sbi[0] < len(split_blocks):
            emit_split_block(*split_blocks[sbi[0]])
            sbi[0] += 1

        for kc in range(4, KC):
            for j in range(4):
                bank = (kc * 4 + j) % 4
                trs([(ps[bank][:, 0:128],
                      aap(j * 1024 + kc * 128, [[16384, 128], [1, 128]]),
                      sb["ident"][:, :])],
                    reads=["A0", "ident"], writes=[f"ps{bank}"])
                v_op("tensor_copy",
                     aap(kc * 1024 + j * 128, [[16384, 128], [1, 128]]),
                     ps[bank][:, 0:128], writes=["A1", f"ps{bank}"])

        for h in range(H):
            buf = "wa" if h % 2 == 0 else "wb"
            q = WAQ if buf == "wa" else WBQ
            dma("Y", wf32(buf, 0, [[128, KC], [1, 128]]),
                ap(wv, h * 128, [[D, 128], [128 * D, KC], [1, 128]]), writes=[q[0]])
            dma("Y", wf32(buf, 2048, [[128, KC], [1, 128]]),
                ap(wk, h * 128, [[D, 128], [128 * D, KC], [1, 128]]), writes=[q[2]])
            for ds in range(KC):
                bank = ds % 2
                items = []
                for kc2 in range(KC):
                    items.append((ps[bank][:, 0:128],
                                  aap(kc2 * 1024 + ds * 128, [[16384, 128], [1, 128]]),
                                  wf32(buf, kc2 * 128, [[1, 128]]),
                                  kc2 == 0, kc2 == KC - 1))
                mm(items, reads=["A0", "A1", q[0]], writes=[f"ps{bank}"])
                v_op("tensor_copy", aap(V_OFF + ds * 128, [[16384, 128], [1, 128]]),
                     ps[bank][:, 0:128], writes=["A2", f"ps{bank}"])
            items = []
            for kc in range(KC):
                items.append((ps[2][:, 0:128],
                              wf32(buf, 2048 + kc * 128, [[1, 128]]),
                              aap(V_OFF + kc * 128, [[16384, 128], [1, 128]]),
                              kc == 0, kc == KC - 1))
            mm(items, reads=["A2", q[2]], writes=["ps2"])
            s = slot_of(h)
            sres = "Sb6" if s < 4 else "Sb7"
            v_op("tensor_copy", sb["Ssb"][:, s * 128:(s + 1) * 128], ps[2][:, 0:128],
                 writes=[sres, "ps2"])

        X0K = [f"x0k{i}" for i in range(KC)]
        X1K = [f"x1k{i}" for i in range(KC)]
        XG_SLOTS = ["sp0", "sp1", "ta0", "ta1"]

        def emit_loads(tl):
            for c in range(CPT):
                dma("Y", ap(sb["stage"], c * 1024, [[4096, 128], [1, 1024]]),
                    ap(xin, (PRE + tl * TT + c * 128) * D, [[D, 128], [1, D]]),
                    writes=[f"st{c}"])

        def make_xg(xsrc, xres_fn, gvec, gres):
            for kc in range(KC):
                spl = XG_SLOTS[kc % 4]
                v_op("tensor_scalar", sb[spl][:, :],
                     xsrc[:, kc * TT:(kc + 1) * TT], gvec[:, kc:kc + 1],
                     None, ALU.mult, reads=[xres_fn(kc), gres], writes=[spl])
                s_act(sb["hhb"][:, kc * TT:(kc + 1) * TT], sb[spl][:, :],
                      AF.Copy, reads=[spl], writes=["hhp0", "hhp1"])
                v_op("tensor_sub", sb["hhb"][:, 4096 + kc * TT:4096 + (kc + 1) * TT],
                     sb[spl][:, :], sb["hhb"][:, kc * TT:(kc + 1) * TT],
                     reads=[spl], writes=["hhp0", "hhp1"])

        def emit_intr_xg(tl):
            for c in range(CPT):
                for kcg in range(2):
                    bank = (c * 2 + kcg) % 4
                    items = []
                    for j in range(4):
                        kc = kcg * 4 + j
                        items.append((ps[bank][:, j * 128:(j + 1) * 128],
                                      ap(sb["stage"], c * 1024 + kc * 128, [[4096, 128], [1, 128]]),
                                      sb["ident"][:, :]))
                    trs(items, reads=[f"st{c}", "ident"], writes=[f"ps{bank}"])
                    v_op("tensor_copy",
                         ap(sb["x0"], kcg * 4 * TT + c * 128, [[4096, 128], [TT, 4], [1, 128]]),
                         ps[bank][:, :],
                         writes=[f"x0k{kcg * 4 + j}" for j in range(4)] + [f"ps{bank}"])
            make_xg(sb["x0"], lambda kc: f"x0k{kc}", sb["g1c"], "g1c")

        def hh_h(kc):
            return sb["hhb"][:, kc * TT:(kc + 1) * TT]

        def hh_l(kc):
            return sb["hhb"][:, 4096 + kc * TT:4096 + (kc + 1) * TT]

        def emit_qkv():
            for (nm, dst_off, dst_res, kind) in [
                ("wq", Q_OFF, "A0", "feat"), ("wk", K_OFF, "A1", "feat"),
                ("wv", V_OFF, "A2", "tok"),
            ]:
                for cb in range(2):
                    buf = "wa" if (cb % 2 == 0) else "wb"
                    q = WAQ if buf == "wa" else WBQ
                    load_hl(nm, cb * 512, 512, KC, D, buf)
                    if kind == "feat":
                        for oc_in in range(4):
                            bank = oc_in % 2
                            items = []
                            k = 0
                            for kc in range(KC):
                                wh = ap(sb[buf], kc * 512 + oc_in * 128, [[8192, 128], [1, 128]])
                                wl = ap(sb[buf], 4096 + kc * 512 + oc_in * 128, [[8192, 128], [1, 128]])
                                for (l, r) in ((wh, hh_h(kc)), (wh, hh_l(kc)), (wl, hh_h(kc))):
                                    items.append((ps[bank][:, :], l, r, k == 0, k == 3 * KC - 1))
                                    k += 1
                            mm(items, reads=["hhp0", "hhp1"] + q, writes=[f"ps{bank}"])
                            oc = cb * 4 + oc_in
                            v_op("tensor_tensor",
                                 aap(dst_off + oc * TT, [[16384, 128], [1, TT]]),
                                 ps[bank][:, :], sb["rrep2"][:, :], ALU.mult,
                                 reads=["rrep2"], writes=[dst_res, f"ps{bank}"])
                    else:
                        for c in range(CPT):
                            bank = c % 2
                            items = []
                            k = 0
                            for kc in range(KC):
                                lh = sb["hhb"][:, kc * TT + c * 128:kc * TT + (c + 1) * 128]
                                ll = sb["hhb"][:, 4096 + kc * TT + c * 128:4096 + kc * TT + (c + 1) * 128]
                                rh = ap(sb[buf], kc * 512, [[8192, 128], [1, 512]])
                                rl = ap(sb[buf], 4096 + kc * 512, [[8192, 128], [1, 512]])
                                for (l, r) in ((lh, rh), (lh, rl), (ll, rh)):
                                    items.append((ps[bank][:, :], l, r, k == 0, k == 3 * KC - 1))
                                    k += 1
                            mm(items, reads=["hhp0", "hhp1"] + q, writes=[f"ps{bank}"])
                            v_op("tensor_scalar",
                                 aap(V_OFF + c * 1024 + cb * 512, [[16384, 128], [1, 512]]),
                                 ps[bank][:, :], sb["rcol"][:, c:c + 1], None, ALU.mult,
                                 reads=["rcol"], writes=[dst_res, f"ps{bank}"])

        def emit_store(tl):
            for c in range(CPT):
                for kcg in range(2):
                    bank = (c * 2 + kcg) % 4
                    items = []
                    for j in range(4):
                        kc = kcg * 4 + j
                        items.append((ps[bank][:, j * 128:(j + 1) * 128],
                                      ap(sb["x1"], kc * TT + c * 128, [[4096, 128], [1, 128]]),
                                      sb["ident"][:, :]))
                    trs(items, reads=X1K + ["ident"], writes=[f"ps{bank}"])
                    s_act(aap(12288 + c * 1024 + kcg * 512, [[16384, 128], [1, 512]]),
                          ps[bank][:, :], AF.Identity,
                          scale=sb["rcol2"][:, c:c + 1], bias=sb["mcol"][:, c:c + 1],
                          reads=["rcol2", "mcol"], writes=["A3", f"ps{bank}"])
                dma("G", ap(y_t, (tl * TT + c * 128) * D, [[D, 128], [1, D]]),
                    aap(12288 + c * 1024, [[16384, 128], [1, 1024]]),
                    reads=["A3"])

        emit_loads(0)
        emit_intr_xg(0)
        stats_rms(sb["x0"], X0K, 4, 5, "rrep2")
        emit_qkv()

        for tl in range(NT):
            for c in range(CPT):
                for hg in range(2):
                    bank = (c * 2 + hg) % 4
                    items = []
                    for j in range(4):
                        h = hg * 4 + j
                        items.append((ps[bank][:, j * 128:(j + 1) * 128],
                                      aap(K_OFF + h * TT + c * 128, [[16384, 128], [1, 128]]),
                                      sb["ident"][:, :]))
                    trs(items, reads=["A1", "ident"], writes=[f"ps{bank}"])
                    s_act(ap(sb["stage"], c * 1024 + hg * 512, [[4096, 128], [1, 512]]),
                          ps[bank][:, :], AF.Copy, writes=[f"st{c}", f"ps{bank}"])

            for i in range(CPT):
                for bg in range(2):
                    bank = (2 - (i % 2) * 2) + bg
                    items = []
                    for j in range(4):
                        h = bg * 4 + j
                        items.append((ps[bank][:, j * 128:(j + 1) * 128],
                                      aap(K_OFF + h * TT + i * 128, [[16384, 128], [1, 128]]),
                                      aap(Q_OFF + h * TT + i * 128, [[16384, 128], [1, 128]]),
                                      True, True))
                    mm(items, reads=["A0", "A1"], writes=[f"ps{bank}"])
                    v_op("tensor_tensor",
                         ap(sb["att"], bg * 512, [[1024, 128], [1, 512]]),
                         ps[bank][:, :],
                         ap(sb["mask"], 0, [[128, 128], [0, 4], [1, 128]]),
                         ALU.mult, reads=["mask"], writes=["att", f"ps{bank}"])
                for bg in range(2):
                    bank = 4 + bg
                    items = []
                    for j in range(4):
                        h = bg * 4 + j
                        s = slot_of(h)
                        items.append((ps[bank][:, j * 128:(j + 1) * 128],
                                      ap(sb["att"], h * 128, [[1024, 128], [1, 128]]),
                                      aap(V_OFF + i * 1024 + h * 128, [[16384, 128], [1, 128]]),
                                      True, False))
                        items.append((ps[bank][:, j * 128:(j + 1) * 128],
                                      aap(Q_OFF + h * TT + i * 128, [[16384, 128], [1, 128]]),
                                      sb["Ssb"][:, s * 128:(s + 1) * 128],
                                      False, True))
                    mm(items, reads=["att", "A0", "A2", "Sb6", "Sb7"],
                       writes=[f"ps{bank}"])
                for bg in range(2):
                    sbank = 6 + bg
                    sitems = []
                    for j in range(4):
                        h = 2 * j + bg
                        sl = slot_of(h) % 4
                        sitems.append((ps[sbank][:, sl * 128:(sl + 1) * 128],
                                       ap(sb["stage"], i * 1024 + h * 128, [[4096, 128], [1, 128]]),
                                       aap(V_OFF + i * 1024 + h * 128, [[16384, 128], [1, 128]]),
                                       True, True))
                    mm(sitems, reads=[f"st{i}", "A2"], writes=[f"ps{sbank}"])
                    sres = "Sb6" if bg == 0 else "Sb7"
                    v_op("tensor_add", sb["Ssb"][:, bg * 512:(bg + 1) * 512],
                         ps[sbank][:, 0:512], sb["Ssb"][:, bg * 512:(bg + 1) * 512],
                         writes=[sres, f"ps{sbank}"])
                for bg in range(2):
                    bank = 4 + bg
                    tsl = "ta0" if bg == 0 else "ta1"
                    s_act(sb[tsl][:, :], ps[bank][:, :], AF.Square,
                          writes=[tsl, f"ps{bank}"])
                    v_op("reduce_sum",
                         ap(sb["ro"], bg * 4, [[8, 128], [1, 4]]),
                         ap(sb[tsl], 0, [[512, 128], [128, 4], [1, 128]]),
                         AX.X, reads=[tsl], writes=["ro"])
                s_act(sb["ro2"][:, :], sb["ro"][:, :], AF.Ln,
                      bias=sb["ceps"][:, 0:1], scale=sb["cDV"][:, 0:1],
                      reads=["ro", "ceps", "cDV"], writes=["ro2"])
                s_act(sb["ro"][:, :], sb["ro2"][:, :], AF.Exp,
                      scale=sb["cNH"][:, 0:1], reads=["ro2", "cNH"], writes=["ro"])
                for bg in range(2):
                    bank = 4 + bg
                    v_op("tensor_tensor",
                         ap(sb["on_"], bg * 512, [[1024, 128], [1, 512]]),
                         ps[bank][:, :],
                         ap(sb["ro"], bg * 4, [[8, 128], [1, 4], [0, 128]]),
                         ALU.mult, reads=["ro"], writes=["on_", f"ps{bank}"])
                for bg in range(2):
                    bank = (2 - (i % 2) * 2) + bg
                    items = []
                    for j in range(4):
                        h = bg * 4 + j
                        items.append((ps[bank][:, j * 128:(j + 1) * 128],
                                      ap(sb["on_"], h * 128, [[1024, 128], [1, 128]]),
                                      sb["ident"][:, :]))
                    trs(items, reads=["on_", "ident"], writes=[f"ps{bank}"])
                    spl = "sp0" if bg == 0 else "sp1"
                    s_act(sb[spl][:, :], ps[bank][:, :], AF.Copy,
                          scale=sb["gocol"][:, 0:1],
                          reads=["gocol"], writes=[spl, f"ps{bank}"])
                    onh = ab16(2 * ONT_H_OFF + bg * 4 * 512 + i * 128,
                               [[32768, 128], [512, 4], [1, 128]])
                    onl = ab16(2 * ONT_L_OFF + bg * 4 * 512 + i * 128,
                               [[32768, 128], [512, 4], [1, 128]])
                    s_act(onh, ap(sb[spl], 0, [[512, 128], [128, 4], [1, 128]]),
                          AF.Copy, reads=[spl], writes=["A3"])
                    v_op("tensor_sub", onl,
                         ap(sb[spl], 0, [[512, 128], [128, 4], [1, 128]]), onh,
                         reads=[spl], writes=["A3"])

            if tl + 1 < NT:
                emit_loads(tl + 1)

            for cb in range(2):
                buf = "wa" if cb % 2 == 0 else "wb"
                q = WAQ if buf == "wa" else WBQ
                load_hl("wo", cb * 512, 512, KC, D, buf)
                for oc_in in range(4):
                    bank = oc_in % 2
                    items = []
                    k = 0
                    for kc in range(KC):
                        wh = ap(sb[buf], kc * 512 + oc_in * 128, [[8192, 128], [1, 128]])
                        wl = ap(sb[buf], 4096 + kc * 512 + oc_in * 128, [[8192, 128], [1, 128]])
                        oh = ab16(2 * ONT_H_OFF + kc * 512, [[32768, 128], [1, 512]])
                        ol = ab16(2 * ONT_L_OFF + kc * 512, [[32768, 128], [1, 512]])
                        for (l, r) in ((wh, oh), (wh, ol), (wl, oh)):
                            items.append((ps[bank][:, :], l, r, k == 0, k == 3 * KC - 1))
                            k += 1
                    mm(items, reads=["A3"] + q, writes=[f"ps{bank}"])
                    oc = cb * 4 + oc_in
                    v_op("tensor_add", sb["x1"][:, oc * TT:(oc + 1) * TT],
                         ps[bank][:, :], sb["x0"][:, oc * TT:(oc + 1) * TT],
                         reads=[f"x0k{oc}"],
                         writes=[f"x1k{oc}", "x1spl", f"ps{bank}"])
                    spl = XG_SLOTS[oc % 4]
                    v_op("tensor_scalar", sb[spl][:, :],
                         sb["x1"][:, oc * TT:(oc + 1) * TT], sb["g2c"][:, oc:oc + 1],
                         None, ALU.mult, reads=[f"x1k{oc}", "g2c"], writes=[spl])
                    s_act(sb["hhb"][:, oc * TT:(oc + 1) * TT], sb[spl][:, :],
                          AF.Copy, reads=[spl], writes=["hhp0", "hhp1"])
                    v_op("tensor_sub", sb["hhb"][:, 4096 + oc * TT:4096 + (oc + 1) * TT],
                         sb[spl][:, :], sb["hhb"][:, oc * TT:(oc + 1) * TT],
                         reads=[spl], writes=["hhp0", "hhp1"])

            stats_rms(sb["x1"], X1K, 4, 5)

            for cb in range(16):
                goff = 0 if cb % 2 == 0 else 2048
                gq = [WAQ[0], WAQ[1]] if cb % 2 == 0 else [WAQ[2], WAQ[3]]
                uq = [WBQ[0], WBQ[1]] if cb % 2 == 0 else [WBQ[2], WBQ[3]]
                dma("Y", ap(sb["wa"], goff * 2, [[8192, 128], [256, KC], [1, 256]]),
                    ap(dwh["wg"], cb * 256, [[INTER, 128], [128 * INTER, KC], [1, 256]]),
                    reads=["Dwgh"], writes=[gq[0]])
                dma("Y", ap(sb["wa"], goff * 2 + 2048, [[8192, 128], [256, KC], [1, 256]]),
                    ap(dwl["wg"], cb * 256, [[INTER, 128], [128 * INTER, KC], [1, 256]]),
                    reads=["Dwgl"], writes=[gq[1]])
                dma("Y", ap(sb["wb"], goff * 2, [[8192, 128], [256, KC], [1, 256]]),
                    ap(dwh["wu"], cb * 256, [[INTER, 128], [128 * INTER, KC], [1, 256]]),
                    reads=["Dwuh"], writes=[uq[0]])
                dma("Y", ap(sb["wb"], goff * 2 + 2048, [[8192, 128], [256, KC], [1, 256]]),
                    ap(dwl["wu"], cb * 256, [[INTER, 128], [128 * INTER, KC], [1, 256]]),
                    reads=["Dwul"], writes=[uq[1]])
                for oc_in in range(2):
                    oc = cb * 2 + oc_in
                    gb = oc_in * 2
                    ub = oc_in * 2 + 1
                    items = []
                    k = 0
                    for kc in range(KC):
                        wh = ap(sb["wa"], goff * 2 + kc * 256 + oc_in * 128, [[8192, 128], [1, 128]])
                        wl = ap(sb["wa"], goff * 2 + 2048 + kc * 256 + oc_in * 128, [[8192, 128], [1, 128]])
                        for (l, r) in ((wh, hh_h(kc)), (wh, hh_l(kc)), (wl, hh_h(kc))):
                            items.append((ps[gb][:, :], l, r, k == 0, k == 3 * KC - 1))
                            k += 1
                    mm(items, reads=["hhp0", "hhp1"] + gq, writes=[f"ps{gb}"])
                    sgs = "sg0" if oc % 2 == 0 else "sg1"
                    sgo = 0 if oc % 2 == 0 else 512
                    spl = "sp0" if oc % 2 == 0 else "sp1"
                    v_op("tensor_tensor", sb[spl][:, :], ps[gb][:, :],
                         sb["rrep"][:, :], ALU.mult,
                         reads=["rrep"], writes=[spl, f"ps{gb}"])
                    s_act(ap(sb["sg"], sgo, [[1024, 128], [1, 512]]), sb[spl][:, :],
                          AF.Silu, reads=[spl], writes=[sgs])
                    items = []
                    k = 0
                    for kc in range(KC):
                        wh = ap(sb["wb"], goff * 2 + kc * 256 + oc_in * 128, [[8192, 128], [1, 128]])
                        wl = ap(sb["wb"], goff * 2 + 2048 + kc * 256 + oc_in * 128, [[8192, 128], [1, 128]])
                        for (l, r) in ((wh, hh_h(kc)), (wh, hh_l(kc)), (wl, hh_h(kc))):
                            items.append((ps[ub][:, :], l, r, k == 0, k == 3 * KC - 1))
                            k += 1
                    mm(items, reads=["hhp0", "hhp1"] + uq, writes=[f"ps{ub}"])
                    tsl = "ta0" if oc % 2 == 0 else "ta1"
                    v_op("tensor_tensor", sb[tsl][:, :], ps[ub][:, :],
                         sb["rrep"][:, :], ALU.mult,
                         reads=["rrep"], writes=[tsl, f"ps{ub}"])
                    v_op("tensor_tensor", sb[spl][:, :],
                         ap(sb["sg"], sgo, [[1024, 128], [1, 512]]), sb[tsl][:, :],
                         ALU.mult, reads=[sgs, tsl], writes=[spl])
                    mres = f"A{oc // 16}"
                    lres = f"A{2 + oc // 16}"
                    mh = ab16(oc * 512, [[32768, 128], [1, 512]])
                    ml = ab16(16384 + oc * 512, [[32768, 128], [1, 512]])
                    s_act(mh, sb[spl][:, :], AF.Copy, reads=[spl], writes=[mres])
                    v_op("tensor_sub", ml, sb[spl][:, :], mh,
                         reads=[spl, mres], writes=[lres])

            for oc2 in range(8):
                buf = "wa" if oc2 % 2 == 0 else "wb"
                q = WAQ if buf == "wa" else WBQ
                dma("Y", ap(sb[buf], 0, [[8192, 128], [128, IC], [1, 128]]),
                    ap(dwh["wd"], oc2 * 128, [[D, 128], [128 * D, IC], [1, 128]]),
                    reads=["Dwdh"], writes=q[0:2])
                dma("Y", ap(sb[buf], 4096, [[8192, 128], [128, IC], [1, 128]]),
                    ap(dwl["wd"], oc2 * 128, [[D, 128], [128 * D, IC], [1, 128]]),
                    reads=["Dwdl"], writes=q[2:4])
                bank = 4 + oc2 % 2
                items = []
                k = 0
                for mc in range(IC):
                    wh = ap(sb[buf], mc * 128, [[8192, 128], [1, 128]])
                    wl = ap(sb[buf], 4096 + mc * 128, [[8192, 128], [1, 128]])
                    mh = ab16(mc * 512, [[32768, 128], [1, 512]])
                    ml = ab16(16384 + mc * 512, [[32768, 128], [1, 512]])
                    for (l, r) in ((wh, mh), (wh, ml), (wl, mh)):
                        items.append((ps[bank][:, :], l, r, k == 0, k == 3 * IC - 1))
                        k += 1
                mm(items, reads=["A0", "A1", "A2", "A3"] + q,
                   writes=[f"ps{bank}"])
                tsl = "ta0" if oc2 % 2 == 0 else "ta1"
                v_op("tensor_add", sb[tsl][:, :], ps[bank][:, :],
                     sb["x1"][:, oc2 * TT:(oc2 + 1) * TT],
                     reads=[f"x1k{oc2}"], writes=[tsl, f"ps{bank}"])
                v_op("tensor_add", sb["x0"][:, oc2 * TT:(oc2 + 1) * TT],
                     sb[tsl][:, :], sb["x0"][:, oc2 * TT:(oc2 + 1) * TT],
                     reads=[tsl], writes=[f"x0k{oc2}"])

            dma("Y", wf32("wa", 0, [[256, KC], [1, 256]]),
                ap(u1, 0, [[RANK, 128], [128 * RANK, KC], [1, 256]]),
                writes=["waq0", "waq1"])
            for oc in range(2):
                bank = 6 + oc
                items = []
                for kc in range(KC):
                    items.append((ps[bank][:, :],
                                  wf32("wa", kc * 256 + oc * 128, [[1, 128]]),
                                  sb["x0"][:, kc * TT:(kc + 1) * TT],
                                  kc == 0, kc == KC - 1))
                mm(items, reads=X0K + ["waq0", "waq1"], writes=[f"ps{bank}"])

            stats_ln(sb["x0"], X0K)

            for oc in range(2):
                bank = 6 + oc
                spl = "sp0" if oc == 0 else "sp1"
                tsl = "ta0" if oc == 0 else "ta1"
                v_op("tensor_tensor", sb[spl][:, :], ps[bank][:, :],
                     sb["rrep"][:, :], ALU.mult,
                     reads=["rrep"], writes=[spl, f"ps{bank}"])
                v_op("scalar_tensor_tensor", sb[tsl][:, :],
                     sb["mrep"][:, :], sb["cu1c"][:, oc:oc + 1], sb[spl][:, :],
                     ALU.mult, ALU.add,
                     reads=["mrep", "cu1c", spl], writes=[tsl])
                s_act(sb["Pb"][:, oc * 512:(oc + 1) * 512], sb[tsl][:, :],
                      AF.Copy, reads=[tsl], writes=["Pb"])
                v_op("tensor_sub", sb["Pb"][:, 1024 + oc * 512:1024 + (oc + 1) * 512],
                     sb[tsl][:, :], sb["Pb"][:, oc * 512:(oc + 1) * 512],
                     reads=[tsl], writes=["Pb"])

            apply_ln(sb["x0"], X0K,
                     lambda kc: aap(X2_OFF + kc * TT, [[16384, 128], [1, TT]]),
                     ["A0"])

            if tl + 1 < NT:
                emit_intr_xg(tl + 1)

            pending_q = []

            def flush_q():
                (qitems, qreads) = pending_q.pop(0)
                mm(qitems, reads=qreads, writes=["ps2", "ps3"])

            for vb in range(4):
                wqa = ["waq0", "waq1"] if vb % 2 == 0 else ["waq2", "waq3"]
                wqb = ["wbq0", "wbq1"] if vb % 2 == 0 else ["wbq2", "wbq3"]
                wao = (vb % 2) * 4096
                dma("Y", ap(sb["wa"], wao, [[8192, 128], [1024, 2], [1, 1024]]),
                    ap(dwh["v1"], vb * 1024, [[INTER, 128], [128 * INTER, 2], [1, 1024]]),
                    reads=["Dv1h"], writes=[wqa[0]])
                dma("Y", ap(sb["wa"], wao + 2048, [[8192, 128], [1024, 2], [1, 1024]]),
                    ap(dwl["v1"], vb * 1024, [[INTER, 128], [128 * INTER, 2], [1, 1024]]),
                    reads=["Dv1l"], writes=[wqa[1]])
                dma("Y", ap(sb["wb"], wao, [[8192, 128], [256, 8], [1, 256]]),
                    ap(dwh["u2"], vb * 1024 * RANK, [[RANK, 128], [128 * RANK, 8], [1, 256]]),
                    reads=["Du2h"], writes=[wqb[0]])
                dma("Y", ap(sb["wb"], wao + 2048, [[8192, 128], [256, 8], [1, 256]]),
                    ap(dwl["u2"], vb * 1024 * RANK, [[RANK, 128], [128 * RANK, 8], [1, 256]]),
                    reads=["Du2l"], writes=[wqb[1]])
                for ocl in range(8):
                    oc = vb * 8 + ocl
                    bank = ocl % 2
                    items = []
                    k = 0
                    for rc in range(2):
                        wh = ap(sb["wa"], wao + rc * 1024 + ocl * 128, [[8192, 128], [1, 128]])
                        wl = ap(sb["wa"], wao + 2048 + rc * 1024 + ocl * 128, [[8192, 128], [1, 128]])
                        ph = sb["Pb"][:, rc * 512:(rc + 1) * 512]
                        pl = sb["Pb"][:, 1024 + rc * 512:1024 + (rc + 1) * 512]
                        for (l, r) in ((wh, ph), (wh, pl), (wl, ph)):
                            items.append((ps[bank][:, :], l, r, k == 0, k == 5))
                            k += 1
                    mm(items, reads=["Pb"] + wqa, writes=[f"ps{bank}"])
                    if len(pending_q) >= 2:
                        flush_q()
                    spl = "sp0" if oc % 2 == 0 else "sp1"
                    s_act(sb[spl][:, :], ps[bank][:, :],
                          AF.Gelu_apprx_tanh, bias=sb["b1c"][:, oc:oc + 1],
                          reads=["b1c"], writes=[spl, f"ps{bank}"])
                    hsl = f"Hb{oc % 3}"
                    hso = (oc % 3) * 512
                    s_act(sb["Hb"][:, hso:hso + 512], sb[spl][:, :],
                          AF.Copy, reads=[spl], writes=[hsl])
                    v_op("tensor_sub", sb["Hb"][:, 1536 + hso:1536 + hso + 512],
                         sb[spl][:, :], sb["Hb"][:, hso:hso + 512],
                         reads=[spl], writes=[hsl])
                    items = []
                    for rc2 in range(2):
                        wh = ap(sb["wb"], wao + ocl * 256 + rc2 * 128, [[8192, 128], [1, 128]])
                        wl = ap(sb["wb"], wao + 2048 + ocl * 256 + rc2 * 128, [[8192, 128], [1, 128]])
                        hhp = sb["Hb"][:, hso:hso + 512]
                        hlp = sb["Hb"][:, 1536 + hso:1536 + hso + 512]
                        k = 0
                        for (l, r) in ((wh, hhp), (wh, hlp), (wl, hhp)):
                            items.append((ps[2 + rc2][:, :], l, r,
                                          oc == 0 and k == 0, oc == IC - 1 and k == 2))
                            k += 1
                    pending_q.append((items, [hsl] + wqb))
            while pending_q:
                flush_q()
            v_op("tensor_copy", aap(QS_OFF, [[16384, 128], [1, 512]]),
                 ps[2][:, :], writes=["A1", "ps2"])
            v_op("tensor_copy", aap(QS_OFF + 512, [[16384, 128], [1, 512]]),
                 ps[3][:, :], writes=["A1", "ps3"])

            dma("Y", wf32("wb", 0, [[1024, 2], [1, 1024]]),
                ap(v2, 0, [[D, 128], [128 * D, 2], [1, 1024]]),
                writes=["wbq0", "wbq1"])
            for oc2 in range(8):
                bank = 4 + oc2 % 2
                items = []
                for rc in range(2):
                    items.append((ps[bank][:, :],
                                  wf32("wb", rc * 1024 + oc2 * 128, [[1, 128]]),
                                  aap(QS_OFF + rc * 512, [[16384, 128], [1, 512]]),
                                  rc == 0, rc == 1))
                mm(items, reads=["A1", "wbq0", "wbq1"], writes=[f"ps{bank}"])
                v_op("scalar_tensor_tensor", sb["x1"][:, oc2 * TT:(oc2 + 1) * TT],
                     ps[bank][:, :], sb["b2c"][:, oc2:oc2 + 1],
                     aap(X2_OFF + oc2 * TT, [[16384, 128], [1, TT]]),
                     ALU.add, ALU.add,
                     reads=["b2c", "A0"], writes=[f"x1k{oc2}", f"ps{bank}"])

            stats_ln(sb["x1"], X1K, bcast=False)
            if tl + 1 < NT:
                stats_rms(sb["x0"], X0K, 4, 5, "rrep2")
                emit_qkv()
            emit_store(tl)

        for s, v in sorted(E.cnt.items()):
            if s.startswith("qr"):
                E.ops["G"].append(("w", s, v))

        sems = {}
        for s in sorted(set(list(E.cnt.keys()) + ["sT", "sV", "sS", "sG"])):
            sems[s] = e(nc.semaphore(s))
        block = e(nc.Block())

        def runner(key):
            def run(ee):
                for item in E.ops[key]:
                    if item[0] == "w":
                        ee.wait_ge(sems[item[1]], item[2])
                    else:
                        _, fn, sem, inc = item
                        fn(ee).then_inc(sems[sem], inc)
            return run

        block.tensor(runner("T"))
        block.vector(runner("V"))
        block.scalar(runner("S"))
        block.gpsimd(runner("G"))
        block.sync(runner("Y"))

    return nc


def kernel(**inputs):
    x = np.ascontiguousarray(np.asarray(inputs["L0_attn_hidden_states"], dtype=np.float32))
    wmap = {
        "wq": "L0_attn_Wq", "wk": "L0_attn_Wk", "wv": "L0_attn_Wv",
        "wo": "L0_attn_Wo", "wg": "L0_attn_Wgate", "wu": "L0_attn_Wup",
        "wd": "L0_attn_Wdown", "u1": "L0_ffn_U1", "v1": "L0_ffn_V1",
        "b1": "L0_ffn_b1", "u2": "L0_ffn_U2", "v2": "L0_ffn_V2",
        "b2": "L0_ffn_b2", "g1": "L0_attn_attn_norm_w",
        "g2": "L0_attn_mlp_norm_w", "go": "L0_attn_attn_out_norm_w",
    }
    ws = {k: np.ascontiguousarray(np.asarray(inputs[v], dtype=np.float32))
          for k, v in wmap.items()}
    nc = build_nc()
    in_maps = []
    for c in range(8):
        b, p = c // 4, c % 4
        xin = np.zeros((TSEQ, D), dtype=np.float32)
        if p > 0:
            xin[PRE - p * TCORE:PRE] = x[b, :p * TCORE]
        xin[PRE:] = x[b, p * TCORE:(p + 1) * TCORE]
        m = {"xin": xin}
        m.update(ws)
        in_maps.append(m)
    res = run_bass_kernel_spmd(nc, in_maps, core_ids=list(range(8)))
    y = np.empty((B, TSEQ, D), dtype=np.float32)
    for c in range(8):
        b, p = c // 4, c % 4
        y[b, p * TCORE:(p + 1) * TCORE] = res.results[c]["y"]
    return y
